# version 17
# baseline (speedup 1.0000x reference)
import numpy as np
import ml_dtypes
from contextlib import ExitStack
import concourse.bass as bass
import concourse.mybir as mybir
from concourse.bass_utils import run_bass_kernel_spmd

F32 = mybir.dt.float32
F16 = mybir.dt.float16
BF16 = mybir.dt.bfloat16
AF = mybir.ActivationFunctionType
ALU = mybir.AluOpType
BF = ml_dtypes.bfloat16

B, L, D, N, R = 2, 4096, 2048, 128, 128
NB, LOC = 16, 4
TCA, NCA = 512, 8
TCS, NCS = 1024, 4
TCC, NCC = 512, 8
HL = 16


def build_program(phases=4, dbg_y=False):
    nc = bass.Bass("TRN2", target_bir_lowering=False, debug=False)
    ap = lambda name, shape, dt_, kind=None: (
        nc.dram_tensor(name, shape, dt_, kind=kind) if kind else nc.dram_tensor(name, shape, dt_))

    XB = ap("x_blk", [NCA * NB * 128, TCA], F32, "ExternalInput").ap()
    XC = ap("xc_blk", [NCC * NB * 128, TCC], F32, "ExternalInput").ap()
    WSEL = ap("wsel_h", [128, NB * NB * 128], F32, "ExternalInput").ap()
    WXP = ap("wxp_h", [128, NB * 3 * 128], F32, "ExternalInput").ap()
    WDT = ap("wdt_h", [128, LOC * 128], F32, "ExternalInput").ap()
    WOUT = ap("wout_h", [128, NB * NB * 128], F32, "ExternalInput").ap()
    WG = ap("wg_h", [128, HL * NB * 128], F32, "ExternalInput").ap()
    WU = ap("wu_h", [128, HL * NB * 128], F32, "ExternalInput").ap()
    WD = ap("wd_h", [128, NB * HL * 128], F32, "ExternalInput").ap()
    NWC = ap("nw_c", [128, NB], F32, "ExternalInput").ap()
    GTC = ap("gate_c", [128, NB], F32, "ExternalInput").ap()
    BSC = ap("bsel_c", [128, NB], F32, "ExternalInput").ap()
    DPC = ap("dp_c", [128, NB], F32, "ExternalInput").ap()
    BDTC = ap("bdt_c", [128, LOC], F32, "ExternalInput").ap()
    ALC = ap("alog_c", [128, LOC * 128], F32, "ExternalInput").ap()
    ONE16 = ap("ones16", [128, 1], BF16, "ExternalInput").ap()
    ONE32 = ap("ones32", [128, 1], F32, "ExternalInput").ap()
    ONER32 = ap("onesr32", [1, 128], F32, "ExternalInput").ap()
    ONER16 = ap("onesr16", [1, 128], BF16, "ExternalInput").ap()
    IDT = ap("ident16", [128, 128], BF16, "ExternalInput").ap()
    OUTQ = ap("out_q", [NB * 128, 1024], F32, "ExternalOutput").ap()
    DBGY = ap("dbg_y", [D, L], BF16, "ExternalOutput").ap() if dbg_y else None
    DBGD = ap("dbg_dt", [128, LOC * L], F16, "ExternalOutput").ap() if dbg_y else None
    DBGX = ap("dbg_dtx", [128, LOC * L], BF16, "ExternalOutput").ap() if dbg_y else None
    DBGB = ap("dbg_bt", [128, L], BF16, "ExternalOutput").ap() if dbg_y else None
    DBGC = ap("dbg_ct", [128, L], BF16, "ExternalOutput").ap() if dbg_y else None
    DBGA = ap("dbg_ac", [128, LOC * 128], F32, "ExternalOutput").ap() if dbg_y else None
    DBGL = ap("dbg_yl", [128, LOC * L], BF16, "ExternalOutput").ap() if dbg_y else None

    WOS = nc.dram_tensor("wout_s", [128, NB * NB * 128], BF16)
    WGS = nc.dram_tensor("wg_s", [128, HL * NB * 128], BF16)
    WUS = nc.dram_tensor("wu_s", [128, HL * NB * 128], BF16)
    WDS = nc.dram_tensor("wd_s", [128, NB * HL * 128], BF16)
    CCI = nc.dram_tensor("cc_in", [LOC * 128, L], BF16)
    CCO = nc.dram_tensor("cc_out", [D, L], BF16)
    RSI = nc.dram_tensor("rs_in", [4 * D, 1024], BF16)
    RSO = nc.dram_tensor("rs_out", [D, 1024], BF16)

    streams = {k: [] for k in ("sync", "scalar", "vector", "tensor", "gpsimd")}
    cnt = {}
    sems = {}

    def sem(name):
        return sems[name]

    def bump(name, k=1):
        cnt[name] = cnt.get(name, 0) + k
        return cnt[name]

    def val(name):
        return cnt.get(name, 0)

    def emit(engname, fn):
        streams[engname].append(fn)

    big = ExitStack()
    sb = lambda n, s, d: big.enter_context(nc.sbuf_tensor(n, s, d))
    for s_ in ("dw", "dgw", "dx", "drs", "t_rs", "t_bc",
               "bs0", "bs1", "bs2", "bs3", "bs4", "bs5", "cs0", "cs1", "cs2", "cs3", "cs4", "cs5",
               "dx0", "dx1", "dx2", "dx3", "wl0", "wl1",
               "wo0", "wo1", "gu0", "gu1", "dcc", "dyc", "dxc", "dln",
               "drsi", "dr2", "dout", "cc",
               "s_sq", "s_rs", "s_mask", "s_dcb", "s_dt", "s_a", "s_cr",
               "s_yev", "s_sg", "s_row", "s_h116",
               "v_xn", "v_xnm", "v_dtx", "v_u", "v_sc", "v_w", "v_z", "v_r1",
               "v_h1", "v_g", "v_pt", "v_var", "v_o", "v_ini",
               "t_ms", "t_sel", "t_xp", "t_dt", "t_y", "t_mx", "t_ln", "t_gu",
               "t_dn"):
        sems[s_] = big.enter_context(nc.semaphore(s_))

    acols = sb("acols", [128, LOC * 128], F32)
    sstate = sb("sstate", [128, LOC * 128], F32)
    nwc = sb("nwc", [128, NB], F32)
    gtc = sb("gtc", [128, NB], F32)
    bsc = sb("bsc", [128, NB], F32)
    gbc = sb("gbc", [128, NB], F32)
    dpc = sb("dpc", [128, NB], F32)
    bdtc = sb("bdtc", [128, LOC], F32)
    one16 = sb("one16", [128, 1], BF16)
    one32 = sb("one32", [128, 1], F32)
    oner32 = sb("oner32", [1, 128], F32)
    ones16b = sb("ones16b", [1, 128], BF16)
    idt = sb("idt", [128, 128], BF16)
    ps = [big.enter_context(nc.psum_tensor(f"ps{i}", [128, 512], F32)) for i in range(8)]

    stage1 = ExitStack()
    sb1 = lambda n, s, d: stage1.enter_context(nc.sbuf_tensor(n, s, d))
    dtT = sb1("dtT", [128, LOC * L], F16)
    dtx16 = sb1("dtx16", [128, LOC * L], BF16)
    BT = sb1("BT", [128, L], BF16)
    CT = sb1("CT", [128, L], BF16)

    def f(sync):
        for dst, src in ((nwc, NWC), (gtc, GTC), (bsc, BSC), (dpc, DPC),
                         (bdtc, BDTC), (one16, ONE16), (one32, ONE32),
                         (oner32, ONER32), (ones16b, ONER16), (idt, IDT), (acols, ALC)):
            sync.dma_start(out=dst[:], in_=src).then_inc(sem("dw"), 16)
    emit("sync", f)
    bump("dw", 16 * 11)
    dw_tok = val("dw")

    def f(gp):
        gp.dma_start(out=wxp16[:], in_=WXP).then_inc(sem("dgw"), 16)
        gp.dma_start(out=wdt16[:], in_=WDT).then_inc(sem("dgw"), 16)
    init_gp = f
    bump("dgw", 16 * 2)
    dgw_init = val("dgw")

    def f(scalar, dw_tok=dw_tok):
        scalar.wait_ge(sem("dw"), dw_tok)
        scalar.activation(out=sstate[:], in_=acols[:], func=AF.Exp,
                          scale=1.0).then_inc(sem("s_row"), 1)
    emit("scalar", f)
    bump("s_row", 1)
    sr_tok = val("s_row")

    def f(vector, dw_tok=dw_tok, sr_tok=sr_tok):
        vector.wait_ge(sem("dw"), dw_tok)
        vector.tensor_tensor(out=gbc[:], in0=gtc[:], in1=bsc[:], op=ALU.mult)
        vector.wait_ge(sem("s_row"), sr_tok)
        vector.tensor_scalar(out=acols[:], in0=sstate[:], scalar1=-1.0,
                             scalar2=None, op0=ALU.mult)
        vector.memset(sstate[:], 0.0)
        vector.engine_nop().then_inc(sem("v_ini"), 1)
    emit("vector", f)
    bump("v_ini", 1)

    stA = ExitStack()
    sbA = lambda n, s, d: stA.enter_context(nc.sbuf_tensor(n, s, d))
    wselb = [sbA(f"wselb{i}", [128, NB * 128], BF16) for i in range(2)]
    wxp16 = sbA("wxp16", [128, NB * 3 * 128], BF16)
    wdt16 = sbA("wdt16", [128, LOC * 128], BF16)
    emit("gpsimd", init_gp)
    xb = [sbA(f"xa{i}", [128, TCA], F32) for i in range(NB)]
    xn16 = [sbA(f"xn{i}", [128, TCA], BF16) for i in range(NB)]
    xnm16 = [sbA(f"xnm{i}", [128, TCA], BF16) for i in range(NB)]
    xsq = [sbA(f"xsq{i}", [128, TCA], BF16) for i in range(2)]
    msk = [sbA(f"msk{i}", [128, TCA], BF16) for i in range(2)]
    d16 = sbA("delta16", [128, TCA], BF16)
    dtcl = [sbA(f"dtcl{i}", [128, TCA], F32) for i in range(2)]
    dtsg = [sbA(f"dtsg{i}", [128, TCA], F32) for i in range(2)]
    dtsg2 = [sbA(f"dtsg2_{i}", [128, TCA], F32) for i in range(2)]
    rsrow = sbA("rsrow", [1, TCA], F32)
    rstmp = sbA("rstmp", [1, TCA], F32)

    for c in range(NCA):
        xn_prev = val("v_xn")
        def f(sync, c=c, xn_need=(xn_prev if c >= 1 else None)):
            if xn_need is not None:
                sync.wait_ge(sem("v_xn"), xn_need)
            for k in range(NB):
                sync.dma_start(out=xb[k][:], in_=XB[(c * NB + k) * 128:(c * NB + k + 1) * 128, :]
                               ).then_inc(sem(f"dx{k % 4}"), 16)
        emit("sync", f)
        for k in range(NB):
            bump(f"dx{k % 4}", 16)
        dxg_tok = {k: val(f"dx{k % 4}") for k in range(NB)}

        tms0 = val("t_ms")
        for k in range(NB):
            def f(scalar, k=k, dxt=dxg_tok[k], need=(tms0 + k - 1), gate=(k >= 2 or c >= 1)):
                scalar.wait_ge(sem(f"dx{k % 4}"), dxt)
                if gate and need > 0:
                    scalar.wait_ge(sem("t_ms"), need)
                scalar.activation(out=xsq[k % 2][:], in_=xb[k][:],
                                  func=AF.Square).then_inc(sem("s_sq"), 1)
            emit("scalar", f)
            bump("s_sq", 1)
            sq_tok = val("s_sq")
            def f(tensor, k=k, sq_tok=sq_tok, xn_free=(16 * c if c >= 1 else None)):
                tensor.wait_ge(sem("s_sq"), sq_tok)
                if k == 0:
                    tensor.wait_ge(sem("v_ini"), 1)
                    if xn_free is not None:
                        tensor.wait_ge(sem("v_xn"), xn_free)
                tensor.matmul(ps[0][0:1, 0:TCA], one16[:], xsq[k % 2][:],
                              start=(k == 0), stop=(k == NB - 1)).then_inc(sem("t_ms"), 1)
            emit("tensor", f)
            bump("t_ms", 1)
        tms_tok = val("t_ms")
        def f(vector, tms_tok=tms_tok):
            vector.wait_ge(sem("t_ms"), tms_tok)
            vector.tensor_scalar(out=rstmp[:], in0=ps[0][0:1, 0:TCA], scalar1=1.0 / D,
                                 scalar2=1e-6, op0=ALU.mult, op1=ALU.add).then_inc(sem("v_var"), 1)
        emit("vector", f)
        bump("v_var", 1)
        vts_tok = val("v_var")
        def f(scalar, vts_tok=vts_tok, mylog=val("s_rs") + 1):
            scalar.wait_ge(sem("v_var"), vts_tok)
            scalar.activation(out=rsrow[:], in_=rstmp[:], func=AF.Ln,
                              scale=1.0).then_inc(sem("s_rs"), 1)
            scalar.wait_ge(sem("s_rs"), mylog)
            scalar.activation(out=rstmp[:], in_=rsrow[:], func=AF.Exp,
                              scale=-0.5).then_inc(sem("s_rs"), 1)
        emit("scalar", f)
        bump("s_rs", 2)
        vrs_tok = val("s_rs")
        def f(tensor, rs_tok=vrs_tok, vts_tok=vts_tok):
            tensor.wait_ge(sem("s_rs"), rs_tok)
            tensor.wait_ge(sem("v_var"), vts_tok)
            tensor.matmul(ps[0][:, 0:TCA], oner32[:], rstmp[0:1, :],
                          start=True, stop=True).then_inc(sem("t_rs"), 1)
        emit("tensor", f)
        bump("t_rs", 1)
        drs_tok = val("t_rs")

        tsel_prev = val("t_sel")
        for k in range(NB):
            def f(vector, k=k, drs_tok=drs_tok, c=c,
                  tsel_need=(tsel_prev if c >= 1 else None)):
                if k == 0:
                    vector.wait_ge(sem("t_rs"), drs_tok)
                    if tsel_need is not None:
                        vector.wait_ge(sem("t_sel"), tsel_need)
                vector.scalar_tensor_tensor(out=xn16[k][:], in0=xb[k][:],
                                            scalar=nwc[:, k:k + 1], in1=ps[0][:, 0:TCA],
                                            op0=ALU.mult, op1=ALU.mult).then_inc(sem("v_xn"), 1)
            emit("vector", f)
            bump("v_xn", 1)
        vxn_tok = val("v_xn")

        for do in range(NB):
            smask0 = val("s_mask")
            wsem = "wl0" if do % 2 == 0 else "wl1"
            def fw(gp, do=do, wsem=wsem, c=c,
                   tsel_need=(val("t_sel") - 1 if (do >= 2 or c >= 1) else None)):
                if tsel_need is not None and tsel_need > 0:
                    gp.wait_ge(sem("t_sel"), tsel_need)
                gp.dma_start(out=wselb[do % 2][:],
                             in_=WSEL[:, do * 2048:(do + 1) * 2048]).then_inc(sem(wsem), 16)
            emit("gpsimd", fw)
            bump(wsem, 16)
            wl_tok = val(wsem)
            def f(tensor, do=do, vxn_tok=vxn_tok, need=(smask0 - 1), wsem=wsem,
                  wl_tok=wl_tok, gate=(do >= 2 or c >= 1)):
                if do == 0:
                    tensor.wait_ge(sem("v_xn"), vxn_tok)
                tensor.wait_ge(sem(wsem), wl_tok)
                if gate and need > 0:
                    tensor.wait_ge(sem("s_mask"), need)
                for di in range(NB):
                    mm = tensor.matmul(ps[1 + do % 2][:, 0:TCA],
                                       wselb[do % 2][:, di * 128:(di + 1) * 128],
                                       xn16[di][:], start=(di == 0), stop=(di == NB - 1))
                    if di == NB - 1:
                        mm.then_inc(sem("t_sel"), 1)
            emit("tensor", f)
            bump("t_sel", 1)
            tsel_tok = val("t_sel")
            def f(scalar, do=do, tsel_tok=tsel_tok):
                scalar.wait_ge(sem("t_sel"), tsel_tok)
                scalar.activation(out=msk[do % 2][:], in_=ps[1 + do % 2][:, 0:TCA],
                                  func=AF.Sigmoid, scale=gtc[:, do:do + 1],
                                  bias=gbc[:, do:do + 1]).then_inc(sem("s_mask"), 1)
            emit("scalar", f)
            bump("s_mask", 1)
            sm_tok = val("s_mask")
            def f(vector, do=do, sm_tok=sm_tok):
                vector.wait_ge(sem("s_mask"), sm_tok)
                vector.tensor_tensor(out=xnm16[do][:], in0=xn16[do][:],
                                     in1=msk[do % 2][:], op=ALU.mult).then_inc(sem("v_xnm"), 1)
            emit("vector", f)
            bump("v_xnm", 1)
        vxnm_tok = val("v_xnm")

        for ob in range(3):
            def f(tensor, ob=ob, vxnm_tok=vxnm_tok, sd_need=(val("s_dcb") - 3 if c >= 1 else None)):
                if ob == 0:
                    tensor.wait_ge(sem("v_xnm"), vxnm_tok)
                    if sd_need is not None:
                        tensor.wait_ge(sem("s_dcb"), sd_need + 3)
                for di in range(NB):
                    mm = tensor.matmul(ps[3 + ob][:, 0:TCA],
                                       wxp16[:, (di * 3 + ob) * 128:(di * 3 + ob + 1) * 128],
                                       xnm16[di][:], start=(di == 0), stop=(di == NB - 1))
                    if di == NB - 1:
                        mm.then_inc(sem("t_xp"), 1)
            emit("tensor", f)
            bump("t_xp", 1)
        txp_tok = val("t_xp")
        def f(scalar, c=c, txp_tok=txp_tok):
            scalar.wait_ge(sem("t_xp"), txp_tok)
            scalar.copy(out=d16[:], in_=ps[3][:, 0:TCA]).then_inc(sem("s_dcb"), 1)
            scalar.copy(out=BT[:, c * TCA:(c + 1) * TCA], in_=ps[4][:, 0:TCA]).then_inc(sem("s_dcb"), 1)
            scalar.copy(out=CT[:, c * TCA:(c + 1) * TCA], in_=ps[5][:, 0:TCA]).then_inc(sem("s_dcb"), 1)
        emit("scalar", f)
        bump("s_dcb", 3)
        sdcb_tok = val("s_dcb")

        for pair in range(2):
            for j in range(2):
                k2 = 2 * pair + j
                def f(tensor, k2=k2, sdcb_tok=sdcb_tok, vcl_need=(val("v_dtx") if c + k2 else None)):
                    if k2 == 0:
                        tensor.wait_ge(sem("s_dcb"), sdcb_tok - 2)
                    if vcl_need is not None:
                        tensor.wait_ge(sem("v_dtx"), vcl_need)
                    tensor.matmul(ps[6 + k2 % 2][:, 0:TCA], wdt16[:, k2 * 128:(k2 + 1) * 128],
                                  d16[:], start=True, stop=True).then_inc(sem("t_dt"), 1)
                emit("tensor", f)
                bump("t_dt", 1)
                tdt_tok = val("t_dt")
                def f(vector, k2=k2, j=j, tdt_tok=tdt_tok):
                    vector.wait_ge(sem("t_dt"), tdt_tok)
                    vector.tensor_scalar(out=dtcl[j][:], in0=ps[6 + k2 % 2][:, 0:TCA],
                                         scalar1=bdtc[:, k2:k2 + 1], scalar2=-80.0,
                                         op0=ALU.add, op1=ALU.max).then_inc(sem("v_dtx"), 1)
                emit("vector", f)
                bump("v_dtx", 1)
            vcl_tok = val("v_dtx")
            def f(scalar, vcl_tok=vcl_tok):
                scalar.wait_ge(sem("v_dtx"), vcl_tok)
                for j in range(2):
                    scalar.activation(out=dtsg2[j][:], in_=dtcl[j][:],
                                      func=AF.Sigmoid, scale=1.0).then_inc(sem("s_dt"), 1)
                for j in range(2):
                    scalar.activation(out=dtsg[j][:], in_=dtsg2[j][:],
                                      func=AF.Ln, scale=1.0).then_inc(sem("s_dt"), 1)
            emit("scalar", f)
            bump("s_dt", 4)
            sdt_tok = val("s_dt")
            for j in range(2):
                k2 = 2 * pair + j
                def f(vector, k2=k2, j=j, c=c, sdt_tok=sdt_tok, mysub=val("v_dtx") + 1):
                    if j == 0:
                        vector.wait_ge(sem("s_dt"), sdt_tok)
                    vector.tensor_tensor(out=dtT[:, k2 * L + c * TCA: k2 * L + (c + 1) * TCA],
                                         in0=dtcl[j][:], in1=dtsg[j][:],
                                         op=ALU.subtract).then_inc(sem("v_dtx"), 1)
                    vector.wait_ge(sem("v_dtx"), mysub)
                    vector.tensor_tensor(out=dtx16[:, k2 * L + c * TCA: k2 * L + (c + 1) * TCA],
                                         in0=dtT[:, k2 * L + c * TCA: k2 * L + (c + 1) * TCA],
                                         in1=xnm16[k2][:], op=ALU.mult).then_inc(sem("v_dtx"), 1)
                emit("vector", f)
                bump("v_dtx", 2)

    a_vdtx = val("v_dtx")
    a_sdcb = val("s_dcb")
    ckpt1 = {k: len(v) for k, v in streams.items()}

    def f(gp):
        PIECES = 4
        for src, dst in ((WOUT, WOS), (WG, WGS), (WU, WUS), (WD, WDS)):
            tot = 32768
            step = tot // PIECES
            for i in range(PIECES):
                gp.dma_start(out=dst.ap()[:, i * step:(i + 1) * step],
                             in_=src[:, i * step:(i + 1) * step]).then_inc(sem("dgw"), 16)
    emit("gpsimd", f)
    bump("dgw", 16 * 16)
    dgw_cast = val("dgw")

    stA.close()

    stS = ExitStack()
    sbS = lambda n, s, d: stS.enter_context(nc.sbuf_tensor(n, s, d))
    sa = [[sbS(f"sa{i}_{k}", [128, TCS], BF16) for k in range(LOC)] for i in range(2)]
    su = [sbS(f"su{i}", [128, TCS], BF16) for i in range(2)]
    sh = [[sbS(f"sh{i}_{k}", [128, TCS], BF16) for k in range(LOC)] for i in range(2)]
    sw = [[sbS(f"sw{i}_{k}", [128, TCS], BF16) for k in range(LOC)] for i in range(2)]
    brepb = [sbS(f"brepb{i}", [128, TCS], BF16) for i in range(6)]
    crepb = [sbS(f"crepb{i}", [128, TCS], BF16) for i in range(6)]
    y16 = sbS("y16", [128, LOC * L], BF16)

    vu_t, vsc_t, vw_t, cr_t, ty_t = {}, {}, {}, {}, {}
    NIT = NCS * 128
    BSEMS = ["bs0", "bs1", "bs2", "bs3", "bs4", "bs5"]
    CSEMS = ["cs0", "cs1", "cs2", "cs3", "cs4", "cs5"]
    bs_tok, cs_tok = {}, {}

    def emit_bcast(jt, first=False):
        cc_, nn_ = jt // 128, jt % 128
        slot = jt % 6
        def fb(sync, jt=jt, cc_=cc_, nn_=nn_, slot=slot, first=first,
               vu_need=(vu_t.get(jt - 6))):
            if first:
                sync.wait_ge(sem("s_dcb"), a_sdcb)
            if vu_need is not None:
                sync.wait_ge(sem("v_u"), vu_need)
            srcb = BT[nn_:nn_ + 1, cc_ * TCS:(cc_ + 1) * TCS].unsqueeze(1).broadcast_to([1, 128, TCS])
            sync.dma_start(out=brepb[slot][:], in_=srcb).then_inc(sem(BSEMS[slot]), 16)
        emit("sync", fb)
        bump(BSEMS[slot], 16)
        bs_tok[jt] = val(BSEMS[slot])
        def fc(gp, jt=jt, cc_=cc_, nn_=nn_, slot=slot, first=first,
               vw_need=(vw_t.get(jt - 6))):
            if first:
                gp.wait_ge(sem("s_dcb"), a_sdcb)
            if vw_need is not None:
                gp.wait_ge(sem("v_w"), vw_need)
            srcc = CT[nn_:nn_ + 1, cc_ * TCS:(cc_ + 1) * TCS].unsqueeze(1).broadcast_to([1, 128, TCS])
            gp.dma_start(out=crepb[slot][:], in_=srcc).then_inc(sem(CSEMS[slot]), 16)
        emit("gpsimd", fc)
        bump(CSEMS[slot], 16)
        cs_tok[jt] = val(CSEMS[slot])

    it = 0
    for c in range(NCS):
        for n in range(128):
            buf = it % 2
            slot = it % 6
            if it == 0:
                for j in range(6):
                    emit_bcast(j, first=(j == 0))

            vsc_need = vsc_t.get((it - 2))
            for k in range(LOC):
                def f(scalar, k=k, n=n, c=c, buf=buf, first=(it == 0),
                      vsc_need=(vsc_need if it >= 2 else None)):
                    if first and k == 0:
                        scalar.wait_ge(sem("v_dtx"), a_vdtx)
                    if k == 0 and vsc_need is not None:
                        scalar.wait_ge(sem("v_sc"), vsc_need)
                    scalar.activation(out=sa[buf][k][:],
                                      in_=dtT[:, k * L + c * TCS: k * L + (c + 1) * TCS],
                                      func=AF.Exp,
                                      scale=acols[:, k * 128 + n:k * 128 + n + 1]
                                      ).then_inc(sem("s_a"), 1)
                emit("scalar", f)
                bump("s_a", 1)
            sa_tok = val("s_a")

            for k in range(LOC):
                col = k * 128 + n
                bump("v_u", 1); vu_t[it] = val("v_u")
                myu = val("v_u")
                bump("v_sc", 1); vsc_t[it] = val("v_sc")
                mysc = val("v_sc")
                bump("v_w", 1); vw_t[it] = val("v_w")
                pten = it - 2
                shb_need = None
                if pten >= 0 and (pten // 128) < NCS - 1:
                    shb_need = cr_t.get((pten // 128, pten % 128, k))
                def f(vector, k=k, n=n, c=c, buf=buf, slot=slot, col=col,
                      sa_tok=sa_tok, bst=bs_tok[it], cst=cs_tok[it],
                      myu=myu, mysc=mysc, shb_need=shb_need,
                      cr_need=(cr_t.get((c - 1, n, k)) if c > 0 else None),
                      ty_need=(ty_t.get((it - 2)) if it >= 2 else None)):
                    if k == 0:
                        vector.wait_ge(sem("s_a"), sa_tok)
                        vector.wait_ge(sem(BSEMS[slot]), bst)
                        vector.wait_ge(sem(CSEMS[slot]), cst)
                        if ty_need is not None:
                            vector.wait_ge(sem("t_y"), ty_need)
                    vector.tensor_tensor(out=su[k % 2][:],
                                         in0=dtx16[:, k * L + c * TCS: k * L + (c + 1) * TCS],
                                         in1=brepb[slot][:], op=ALU.mult).then_inc(sem("v_u"), 1)
                    vector.wait_ge(sem("v_u"), myu)
                    if cr_need is not None:
                        vector.wait_ge(sem("s_cr"), cr_need)
                    if shb_need is not None:
                        vector.wait_ge(sem("s_cr"), shb_need)
                    vector.tensor_tensor_scan(out=sh[buf][k][:], data0=sa[buf][k][:],
                                              data1=su[k % 2][:],
                                              initial=sstate[:, col:col + 1],
                                              op0=ALU.mult, op1=ALU.add).then_inc(sem("v_sc"), 1)
                    vector.wait_ge(sem("v_sc"), mysc)
                    vector.tensor_tensor(out=sw[buf][k][:], in0=sh[buf][k][:],
                                         in1=crepb[slot][:], op=ALU.mult).then_inc(sem("v_w"), 1)
                emit("vector", f)
                vwk = val("v_w")

                if c < NCS - 1:
                    def f(gp, k=k, buf=buf, col=col, vst=vsc_t[it]):
                        gp.wait_ge(sem("v_sc"), vst)
                        gp.tensor_copy(out=sstate[:, col:col + 1],
                                       in_=sh[buf][k][:, TCS - 1:TCS]).then_inc(sem("s_cr"), 1)
                    emit("gpsimd", f)
                    bump("s_cr", 1)
                    cr_t[(c, n, k)] = val("s_cr")

                def f(tensor, k=k, n=n, buf=buf, vwk=vwk,
                      yev_need=(val("s_yev") if (n == 0 and c > 0 and k == 0) else None)):
                    tensor.wait_ge(sem("v_w"), vwk)
                    if yev_need is not None:
                        tensor.wait_ge(sem("s_yev"), yev_need)
                    for s in range(2):
                        mm = tensor.matmul(ps[2 * k + s][:], idt[:],
                                           sw[buf][k][:, s * 512:(s + 1) * 512],
                                           start=(n == 0), stop=(n == 127))
                        if s == 1:
                            mm.then_inc(sem("t_y"), 1)
                emit("tensor", f)
                bump("t_y", 1)
            ty_t[it] = val("t_y")
            if it + 6 < NIT:
                emit_bcast(it + 6)
            it += 1

        ty_tok = val("t_y")
        def f(scalar, c=c, ty_tok=ty_tok):
            scalar.wait_ge(sem("t_y"), ty_tok)
            for k in range(LOC):
                for s in range(2):
                    scalar.copy(out=y16[:, k * L + c * TCS + s * 512: k * L + c * TCS + (s + 1) * 512],
                                in_=ps[2 * k + s][:]).then_inc(sem("s_yev"), 1)
        emit("scalar", f)
        bump("s_yev", 2 * LOC)

    yev_all = val("s_yev")
    def f(gp, yev_all=yev_all):
        gp.wait_ge(sem("s_yev"), yev_all)
        if DBGY is not None:
            gp.dma_start(out=DBGD, in_=dtT[:]).then_inc(sem("dcc"), 16)
            gp.dma_start(out=DBGX, in_=dtx16[:]).then_inc(sem("dcc"), 16)
            gp.dma_start(out=DBGB, in_=BT[:]).then_inc(sem("dcc"), 16)
            gp.dma_start(out=DBGC, in_=CT[:]).then_inc(sem("dcc"), 16)
            gp.dma_start(out=DBGA, in_=acols[:]).then_inc(sem("dcc"), 16)
            gp.dma_start(out=DBGL, in_=y16[:]).then_inc(sem("dcc"), 16)
            gp.wait_ge(sem("dcc"), 96)
        base_dcc = 96 if DBGY is not None else 0
        for k in range(LOC):
            gp.dma_start(out=CCI.ap()[k * 128:(k + 1) * 128, :],
                         in_=y16[:, k * L:(k + 1) * L]).then_inc(sem("dcc"), 16)
        gp.wait_ge(sem("dcc"), base_dcc + 16 * LOC)
        gp.collective_compute("AllGather", ALU.bypass,
                              replica_groups=[[0, 1, 2, 3], [4, 5, 6, 7]],
                              ins=[CCI.ap().opt()],
                              outs=[CCO.ap().opt()]).then_inc(sem("cc"), 1)
        gp.wait_ge(sem("cc"), 1)
        if DBGY is not None:
            gp.dma_start(out=DBGY, in_=CCO.ap()).then_inc(sem("dcc"), 16)
    emit("gpsimd", f)
    bump("dcc", 16 * LOC)
    bump("cc", 1)

    ckpt3 = {k: len(v) for k, v in streams.items()}
    stS.close()
    stage1.close()

    stC = ExitStack()
    sbC = lambda n, s, d: stC.enter_context(nc.sbuf_tensor(n, s, d))
    xc = [sbC(f"xc{i}", [128, TCC], F32) for i in range(NB)]
    yc = [sbC(f"yc{i}", [128, TCC], BF16) for i in range(NB)]
    z16 = [sbC(f"z{i}", [128, TCC], BF16) for i in range(NB)]
    r1 = [sbC(f"r1_{i}", [128, TCC], F32) for i in range(NB)]
    h1 = [sbC(f"h1_{i}", [128, TCC], F32) for i in range(NB)]
    h116 = [sbC(f"h116_{i}", [128, TCC], BF16) for i in range(NB)]
    gated = [sbC(f"gat{i}", [128, TCC], BF16) for i in range(HL)]
    wob = [sbC(f"wob{i}", [128, NB * 128], BF16) for i in range(2)]
    wgb = [sbC(f"wgb{i}", [128, NB * 128], BF16) for i in range(2)]
    wub = [sbC(f"wub{i}", [128, NB * 128], BF16) for i in range(2)]
    sg16 = [sbC("sg0", [128, TCC], BF16)]
    r1sq = [sbC("rsq0", [128, TCC], BF16)]
    tmp32 = sbC("tmp32", [128, TCC], F32)
    mrow = sbC("mrow", [1, TCC], F32)
    msq = sbC("msq", [1, TCC], F32)
    s2row = sbC("s2row", [1, TCC], F32)
    varrow = sbC("varrow", [1, TCC], F32)
    rstdrow = sbC("rstdrow", [1, TCC], F32)
    mrep = sbC("mrep", [128, TCC], F32)
    rstdrep = sbC("rstdrep", [128, TCC], F32)

    for t in range(NCC):
        quarter, half = t // 2, t % 2
        def f(sync, t=t, vr1_need=(val("v_r1") if t >= 1 else None),
              vz_need=(val("v_z") if t >= 1 else None),
              drsi_need=(val("drsi") if t >= 1 else None)):
            if t == 0:
                sync.wait_ge(sem("cc"), 1)
            if vr1_need is not None:
                sync.wait_ge(sem("v_r1"), vr1_need)
            if drsi_need is not None:
                sync.wait_ge(sem("dout" if False else "drsi"), drsi_need)
            for k in range(NB):
                sync.dma_start(out=xc[k][:], in_=XC[(t * NB + k) * 128:(t * NB + k + 1) * 128, :]
                               ).then_inc(sem("dxc"), 16)
            if vz_need is not None:
                sync.wait_ge(sem("v_z"), vz_need)
            for k in range(NB):
                sync.dma_start(out=yc[k][:], in_=CCO.ap()[k * 128:(k + 1) * 128, t * TCC:(t + 1) * TCC]
                               ).then_inc(sem("dyc"), 16)
        emit("sync", f)
        bump("dxc", 16 * NB)
        bump("dyc", 16 * NB)
        dxc_tok, dyc_tok = val("dxc"), val("dyc")

        for do in range(NB):
            def f(gp, do=do, t=t, first=(t == 0 and do == 0),
                  tmx_need=(val("t_mx") - 1 if (t * NB + do) >= 2 else None)):
                if first:
                    gp.wait_ge(sem("dgw"), dgw_cast)
                if tmx_need is not None:
                    gp.wait_ge(sem("t_mx"), tmx_need)
                gp.dma_start(out=wob[do % 2][:], in_=WOS.ap()[:, do * 2048:(do + 1) * 2048]
                             ).then_inc(sem("wo0" if do % 2 == 0 else "wo1"), 16)
            emit("gpsimd", f)
            wsem = "wo0" if do % 2 == 0 else "wo1"
            bump(wsem, 16)
            dgw_tok = val(wsem)

            if do < NB:
                pass
            if do == 0:
                for k in range(NB):
                    def f(vector, k=k, dxc_tok=dxc_tok, dyc_tok=dyc_tok):
                        if k == 0:
                            vector.wait_ge(sem("dxc"), dxc_tok)
                            vector.wait_ge(sem("dyc"), dyc_tok)
                        vector.scalar_tensor_tensor(out=z16[k][:], in0=xc[k][:],
                                                    scalar=dpc[:, k:k + 1], in1=yc[k][:],
                                                    op0=ALU.mult, op1=ALU.add).then_inc(sem("v_z"), 1)
                    emit("vector", f)
                    bump("v_z", 1)
            vz_tok = val("v_z")

            def f(tensor, do=do, vz_tok=vz_tok, dgw_tok=dgw_tok, wsem=wsem,
                  vr1_need=(val("v_r1") - 1 if do >= 2 else None),
                  vpt_need=(val("v_pt") if (do < 2 and t >= 1) else None)):
                if do == 0:
                    tensor.wait_ge(sem("v_z"), vz_tok)
                tensor.wait_ge(sem(wsem), dgw_tok)
                if vr1_need is not None:
                    tensor.wait_ge(sem("v_r1"), vr1_need)
                if vpt_need is not None:
                    tensor.wait_ge(sem("v_pt"), vpt_need)
                for di in range(NB):
                    mm = tensor.matmul(ps[do % 2][:], wob[do % 2][:, di * 128:(di + 1) * 128],
                                       z16[di][:], start=(di == 0), stop=(di == NB - 1))
                    if di == NB - 1:
                        mm.then_inc(sem("t_mx"), 1)
            emit("tensor", f)
            bump("t_mx", 1)
            tmx_tok = val("t_mx")
            def f(vector, do=do, tmx_tok=tmx_tok,
                  drsi_need=(val("drsi") if (do == 0 and t >= 1) else None)):
                if drsi_need is not None:
                    vector.wait_ge(sem("drsi"), drsi_need)
                vector.wait_ge(sem("t_mx"), tmx_tok)
                vector.tensor_tensor(out=r1[do][:], in0=ps[do % 2][:], in1=xc[do][:],
                                     op=ALU.add).then_inc(sem("v_r1"), 1)
            emit("vector", f)
            bump("v_r1", 1)

        vr1_all = val("v_r1")
        for k in range(NB):
            def f(scalar, k=k, vr1_all=vr1_all, tln_need=(val("t_ln") if (k >= 1 or t >= 1) else None)):
                if k == 0:
                    scalar.wait_ge(sem("v_r1"), vr1_all)
                if tln_need is not None:
                    scalar.wait_ge(sem("t_ln"), max(0, tln_need))
                scalar.activation(out=r1sq[0][:], in_=r1[k][:],
                                  func=AF.Square).then_inc(sem("s_sq"), 1)
            emit("scalar", f)
            bump("s_sq", 1)
            ssq_tok = val("s_sq")
            def f(tensor, k=k, vr1_all=vr1_all, ssq_tok=ssq_tok,
                  srow_need=(val("s_row") if (k == 0 and t >= 1) else None)):
                if k == 0:
                    tensor.wait_ge(sem("v_r1"), vr1_all)
                    if srow_need is not None:
                        tensor.wait_ge(sem("s_row"), srow_need)
                tensor.matmul(ps[2][0:1, :], one32[:], r1[k][:],
                              start=(k == 0), stop=(k == NB - 1))
                tensor.wait_ge(sem("s_sq"), ssq_tok)
                mm = tensor.matmul(ps[3][0:1, :], one16[:], r1sq[0][:],
                                   start=(k == 0), stop=(k == NB - 1))
                mm.then_inc(sem("t_ln"), 1)
            emit("tensor", f)
            bump("t_ln", 1)
        tln_tok = val("t_ln")
        def f(scalar, tln_tok=tln_tok):
            scalar.wait_ge(sem("t_ln"), tln_tok)
            scalar.activation(out=mrow[:], in_=ps[2][0:1, :], func=AF.Copy, scale=1.0 / D)
            scalar.activation(out=s2row[:], in_=ps[3][0:1, :], func=AF.Copy, scale=1.0 / D)
            scalar.activation(out=msq[:], in_=mrow[:], func=AF.Square).then_inc(sem("s_row"), 1)
        emit("scalar", f)
        bump("s_row", 1)
        srow_tok = val("s_row")
        def f(vector, srow_tok=srow_tok, myv=val("v_var") + 1):
            vector.wait_ge(sem("s_row"), srow_tok)
            vector.tensor_tensor(out=varrow[:], in0=s2row[:], in1=msq[:],
                                 op=ALU.subtract).then_inc(sem("v_var"), 1)
            vector.wait_ge(sem("v_var"), myv)
            vector.tensor_scalar(out=msq[:], in0=varrow[:], scalar1=1e-5,
                                 scalar2=None, op0=ALU.add).then_inc(sem("v_var"), 1)
        emit("vector", f)
        bump("v_var", 2)
        vvar_tok = val("v_var")
        def f(scalar, vvar_tok=vvar_tok, mylog=val("s_row") + 1):
            scalar.wait_ge(sem("v_var"), vvar_tok)
            scalar.activation(out=varrow[:], in_=msq[:], func=AF.Ln,
                              scale=1.0).then_inc(sem("s_row"), 1)
            scalar.wait_ge(sem("s_row"), mylog)
            scalar.activation(out=rstdrow[:], in_=varrow[:], func=AF.Exp,
                              scale=-0.5).then_inc(sem("s_row"), 1)
        emit("scalar", f)
        bump("s_row", 2)
        vvar2_tok = val("s_row")
        def f(sync, srow_tok=vvar2_tok):
            sync.wait_ge(sem("s_row"), srow_tok)
            sync.dma_start(out=mrep[:], in_=mrow[0:1, :].unsqueeze(1).broadcast_to([1, 128, TCC])
                           ).then_inc(sem("dln"), 16)
            sync.dma_start(out=rstdrep[:], in_=rstdrow[0:1, :].unsqueeze(1).broadcast_to([1, 128, TCC])
                           ).then_inc(sem("dln"), 16)
        emit("sync", f)
        bump("dln", 32)
        dln_tok = val("dln")

        for k in range(NB):
            bump("v_h1", 1)
            myh = val("v_h1")
            def f(vector, k=k, dln_tok=dln_tok, myh=myh):
                if k == 0:
                    vector.wait_ge(sem("dln"), dln_tok)
                vector.tensor_tensor(out=tmp32[:], in0=r1[k][:], in1=mrep[:],
                                     op=ALU.subtract).then_inc(sem("v_h1"), 1)
                vector.wait_ge(sem("v_h1"), myh)
                vector.tensor_tensor(out=h1[k][:], in0=tmp32[:], in1=rstdrep[:],
                                     op=ALU.mult).then_inc(sem("v_h1"), 1)
            emit("vector", f)
            bump("v_h1", 1)
            vh_tok = val("v_h1")
            def f(scalar, k=k, vh_tok=vh_tok):
                scalar.wait_ge(sem("v_h1"), vh_tok)
                scalar.copy(out=h116[k][:], in_=h1[k][:]).then_inc(sem("s_h116"), 1)
            emit("scalar", f)
            bump("s_h116", 1)
        sh116_tok = val("s_h116")

        for hl in range(HL):
            def f(gp, hl=hl, t=t, tgu_need=(val("t_gu") - 2 if (t * HL + hl) >= 2 else None),
                  tdn_need=(val("t_dn") if (t >= 1 and hl < 2) else None)):
                if tgu_need is not None:
                    gp.wait_ge(sem("t_gu"), max(0, tgu_need))
                if tdn_need is not None:
                    gp.wait_ge(sem("t_dn"), tdn_need)
                gsem_ = "gu0" if hl % 2 == 0 else "gu1"
                gp.dma_start(out=wgb[hl % 2][:], in_=WGS.ap()[:, hl * 2048:(hl + 1) * 2048]
                             ).then_inc(sem(gsem_), 16)
            emit("gpsimd", f)
            def f2(eng, hl=hl, t=t, tgu_need=(val("t_gu") - 2 if (t * HL + hl) >= 2 else None)):
                if tgu_need is not None:
                    eng.wait_ge(sem("t_gu"), max(0, tgu_need))
                gsem_ = "gu0" if hl % 2 == 0 else "gu1"
                eng.dma_start(out=wub[hl % 2][:], in_=WUS.ap()[:, hl * 2048:(hl + 1) * 2048]
                              ).then_inc(sem(gsem_), 16)
            emit("sync", f2)
            gsem = "gu0" if hl % 2 == 0 else "gu1"
            bump(gsem, 32)
            dgw_tok = val(gsem)
            def f(tensor, hl=hl, dgw_tok=dgw_tok, gsem=gsem, sh116_tok=sh116_tok,
                  ssg_need=(val("s_sg") - 1 if (hl >= 2 or t >= 1) else None),
                  vg_need=(val("v_g") - 1 if (hl >= 2 or t >= 1) else None)):
                if hl == 0:
                    tensor.wait_ge(sem("s_h116"), sh116_tok)
                tensor.wait_ge(sem(gsem), dgw_tok)
                if ssg_need is not None:
                    tensor.wait_ge(sem("s_sg"), max(0, ssg_need))
                for di in range(NB):
                    mm = tensor.matmul(ps[4 + hl % 2][:], wgb[hl % 2][:, di * 128:(di + 1) * 128],
                                       h116[di][:], start=(di == 0), stop=(di == NB - 1))
                    if di == NB - 1:
                        mm.then_inc(sem("t_gu"), 1)
                if vg_need is not None:
                    tensor.wait_ge(sem("v_g"), max(0, vg_need))
                for di in range(NB):
                    mm = tensor.matmul(ps[6 + hl % 2][:], wub[hl % 2][:, di * 128:(di + 1) * 128],
                                       h116[di][:], start=(di == 0), stop=(di == NB - 1))
                    if di == NB - 1:
                        mm.then_inc(sem("t_gu"), 1)
            emit("tensor", f)
            bump("t_gu", 2)
            tgu_tok = val("t_gu")
            def f(scalar, hl=hl, tgu_tok=tgu_tok, vg_prev=(val("v_g") if (hl >= 1 or t >= 1) else None)):
                scalar.wait_ge(sem("t_gu"), tgu_tok - 1)
                if vg_prev is not None:
                    scalar.wait_ge(sem("v_g"), vg_prev)
                scalar.activation(out=sg16[0][:], in_=ps[4 + hl % 2][:],
                                  func=AF.Silu).then_inc(sem("s_sg"), 1)
            emit("scalar", f)
            bump("s_sg", 1)
            ssg_tok = val("s_sg")
            def f(vector, hl=hl, ssg_tok=ssg_tok, tgu_tok=tgu_tok):
                vector.wait_ge(sem("s_sg"), ssg_tok)
                vector.wait_ge(sem("t_gu"), tgu_tok)
                vector.tensor_tensor(out=gated[hl][:], in0=sg16[0][:],
                                     in1=ps[6 + hl % 2][:], op=ALU.mult).then_inc(sem("v_g"), 1)
            emit("vector", f)
            bump("v_g", 1)
        vg_all = val("v_g")

        tgu_all = val("t_gu")
        for do in range(NB):
            def f(gp, do=do, tdn_need=(val("t_dn") - 1 if do >= 2 else None), tgu_all=tgu_all):
                if do < 2:
                    gp.wait_ge(sem("t_gu"), tgu_all)
                if tdn_need is not None:
                    gp.wait_ge(sem("t_dn"), max(0, tdn_need))
                gsem_ = "gu0" if do % 2 == 0 else "gu1"
                gp.dma_start(out=wgb[do % 2][:], in_=WDS.ap()[:, do * 2048:(do + 1) * 2048]
                             ).then_inc(sem(gsem_), 16)
            emit("gpsimd", f)
            gsem = "gu0" if do % 2 == 0 else "gu1"
            bump(gsem, 16)
            dgw_tok = val(gsem)
            def f(tensor, do=do, vg_all=vg_all, dgw_tok=dgw_tok, gsem=gsem,
                  vpt_need=(val("v_pt") - 1 if do >= 2 else None)):
                if do == 0:
                    tensor.wait_ge(sem("v_g"), vg_all)
                tensor.wait_ge(sem(gsem), dgw_tok)
                if vpt_need is not None:
                    tensor.wait_ge(sem("v_pt"), max(0, vpt_need))
                for hl in range(HL):
                    mm = tensor.matmul(ps[do % 2][:], wgb[do % 2][:, hl * 128:(hl + 1) * 128],
                                       gated[hl][:], start=(hl == 0), stop=(hl == HL - 1))
                    if hl == HL - 1:
                        mm.then_inc(sem("t_dn"), 1)
            emit("tensor", f)
            bump("t_dn", 1)
            tdn_tok = val("t_dn")
            def f(vector, do=do, tdn_tok=tdn_tok):
                vector.wait_ge(sem("t_dn"), tdn_tok)
                vector.scalar_tensor_tensor(out=z16[do][:], in0=h1[do][:], scalar=0.25,
                                            in1=ps[do % 2][:], op0=ALU.mult,
                                            op1=ALU.add).then_inc(sem("v_pt"), 1)
            emit("vector", f)
            bump("v_pt", 1)
            vpt_tok = val("v_pt")
            def f(sync, do=do, t=t, quarter=quarter, half=half, vpt_tok=vpt_tok):
                sync.wait_ge(sem("v_pt"), vpt_tok)
                sync.dma_start(out=RSI.ap()[quarter * D + do * 128:quarter * D + (do + 1) * 128,
                                            half * TCC:(half + 1) * TCC],
                               in_=z16[do][:]).then_inc(sem("drsi"), 16)
            emit("sync", f)
            bump("drsi", 16)

    drsi_all = val("drsi")
    def f(gp, drsi_all=drsi_all):
        gp.wait_ge(sem("drsi"), drsi_all)
        gp.collective_compute("ReduceScatter", ALU.add,
                              replica_groups=[[0, 1, 2, 3], [4, 5, 6, 7]],
                              ins=[RSI.ap().opt()],
                              outs=[RSO.ap().opt()]).then_inc(sem("cc"), 1)
        gp.wait_ge(sem("cc"), 2)
    emit("gpsimd", f)
    bump("cc", 1)

    for hh in range(2):
        def f(sync, hh=hh, vo_need=(val("v_o") if hh == 1 else None)):
            if hh == 0:
                sync.wait_ge(sem("cc"), 2)
            if vo_need is not None:
                sync.wait_ge(sem("v_o"), vo_need)
            for k in range(NB):
                sync.dma_start(out=yc[k][:], in_=RSO.ap()[k * 128:(k + 1) * 128,
                                                          hh * TCC:(hh + 1) * TCC]
                               ).then_inc(sem("dr2"), 16)
        emit("sync", f)
        bump("dr2", 16 * NB)
        dr2_tok = val("dr2")
        for k in range(NB):
            def f(scalar, k=k, dr2_tok=dr2_tok, tln_need=(val("t_ln") if (k >= 1 or hh >= 0) else None)):
                if k == 0:
                    scalar.wait_ge(sem("dr2"), dr2_tok)
                if tln_need is not None:
                    scalar.wait_ge(sem("t_ln"), max(0, tln_need))
                scalar.activation(out=r1sq[0][:], in_=yc[k][:],
                                  func=AF.Square).then_inc(sem("s_sq"), 1)
            emit("scalar", f)
            bump("s_sq", 1)
            ssq_tok = val("s_sq")
            def f(tensor, k=k, dr2_tok=dr2_tok, ssq_tok=ssq_tok):
                if k == 0:
                    tensor.wait_ge(sem("dr2"), dr2_tok)
                tensor.matmul(ps[2][0:1, :], one16[:], yc[k][:],
                              start=(k == 0), stop=(k == NB - 1))
                tensor.wait_ge(sem("s_sq"), ssq_tok)
                mm = tensor.matmul(ps[3][0:1, :], one16[:], r1sq[0][:],
                                   start=(k == 0), stop=(k == NB - 1))
                mm.then_inc(sem("t_ln"), 1)
            emit("tensor", f)
            bump("t_ln", 1)
        tln_tok = val("t_ln")
        def f(scalar, tln_tok=tln_tok):
            scalar.wait_ge(sem("t_ln"), tln_tok)
            scalar.activation(out=mrow[:], in_=ps[2][0:1, :], func=AF.Copy, scale=1.0 / D)
            scalar.activation(out=s2row[:], in_=ps[3][0:1, :], func=AF.Copy, scale=1.0 / D)
            scalar.activation(out=msq[:], in_=mrow[:], func=AF.Square).then_inc(sem("s_row"), 1)
        emit("scalar", f)
        bump("s_row", 1)
        srow_tok = val("s_row")
        def f(vector, srow_tok=srow_tok, myv=val("v_var") + 1):
            vector.wait_ge(sem("s_row"), srow_tok)
            vector.tensor_tensor(out=varrow[:], in0=s2row[:], in1=msq[:],
                                 op=ALU.subtract).then_inc(sem("v_var"), 1)
            vector.wait_ge(sem("v_var"), myv)
            vector.tensor_scalar(out=msq[:], in0=varrow[:], scalar1=1e-5,
                                 scalar2=None, op0=ALU.add).then_inc(sem("v_var"), 1)
        emit("vector", f)
        bump("v_var", 2)
        vvar_tok = val("v_var")
        def f(scalar, vvar_tok=vvar_tok, mylog=val("s_row") + 1):
            scalar.wait_ge(sem("v_var"), vvar_tok)
            scalar.activation(out=varrow[:], in_=msq[:], func=AF.Ln,
                              scale=1.0).then_inc(sem("s_row"), 1)
            scalar.wait_ge(sem("s_row"), mylog)
            scalar.activation(out=rstdrow[:], in_=varrow[:], func=AF.Exp,
                              scale=-0.5).then_inc(sem("s_row"), 1)
        emit("scalar", f)
        bump("s_row", 2)
        vvar2_tok = val("s_row")
        def f(sync, srow_tok=vvar2_tok):
            sync.wait_ge(sem("s_row"), srow_tok)
            sync.dma_start(out=mrep[:], in_=mrow[0:1, :].unsqueeze(1).broadcast_to([1, 128, TCC])
                           ).then_inc(sem("dln"), 16)
            sync.dma_start(out=rstdrep[:], in_=rstdrow[0:1, :].unsqueeze(1).broadcast_to([1, 128, TCC])
                           ).then_inc(sem("dln"), 16)
        emit("sync", f)
        bump("dln", 32)
        dln_tok = val("dln")
        for k in range(NB):
            bump("v_o", 1)
            myo = val("v_o")
            def f(vector, k=k, dln_tok=dln_tok, myo=myo,
                  dout_need=(val("dout") if (k == 0 and hh == 1) else None)):
                if k == 0:
                    vector.wait_ge(sem("dln"), dln_tok)
                    if dout_need is not None:
                        vector.wait_ge(sem("dout"), dout_need)
                vector.tensor_tensor(out=tmp32[:], in0=yc[k][:], in1=mrep[:],
                                     op=ALU.subtract).then_inc(sem("v_o"), 1)
                vector.wait_ge(sem("v_o"), myo)
                vector.tensor_tensor(out=h1[k][:], in0=tmp32[:], in1=rstdrep[:],
                                     op=ALU.mult).then_inc(sem("v_o"), 1)
            emit("vector", f)
            bump("v_o", 1)
            vo_tok = val("v_o")
            def f(sync, k=k, hh=hh, vo_tok=vo_tok):
                sync.wait_ge(sem("v_o"), vo_tok)
                sync.dma_start(out=OUTQ[k * 128:(k + 1) * 128, hh * TCC:(hh + 1) * TCC],
                               in_=h1[k][:]).then_inc(sem("dout"), 16)
            emit("sync", f)
            bump("dout", 16)

    dout_all = val("dout")
    def f(sync, dout_all=dout_all):
        sync.wait_ge(sem("dout"), dout_all)
    emit("sync", f)
    def f(gp, dout_all=dout_all):
        gp.wait_ge(sem("dout"), dout_all)
    emit("gpsimd", f)

    stC.close()

    if phases < 4:
        ck = {1: ckpt1, 2: ckpt2, 3: ckpt3}[phases]
        for k in streams:
            streams[k] = streams[k][:ck[k]]

    with nc.Block() as block:
        @block.sync
        def _(sync):
            for fn in streams["sync"]:
                fn(sync)

        @block.scalar
        def _(scalar):
            for fn in streams["scalar"]:
                fn(scalar)

        @block.vector
        def _(vector):
            for fn in streams["vector"]:
                fn(vector)

        @block.tensor
        def _(tensor):
            for fn in streams["tensor"]:
                fn(tensor)

        @block.gpsimd
        def _(gpsimd):
            for fn in streams["gpsimd"]:
                fn(gpsimd)

    big.close()
    return nc


def kernel(L0_attn_x, L0_attn_norm_w, L0_attn_select_proj_w, L0_attn_select_proj_b,
           L0_attn_select_gate, L0_attn_x_proj_w, L0_attn_dt_proj_w, L0_attn_dt_proj_b,
           L0_attn_A_log, L0_attn_D_param, L0_attn_out_proj_w, L0_ffn_w_gate,
           L0_ffn_w_up, L0_ffn_w_down):
    f32 = np.float32
    x = np.asarray(L0_attn_x, f32)
    nw = np.asarray(L0_attn_norm_w, f32)
    Wsel = np.asarray(L0_attn_select_proj_w, f32)
    bsel = np.asarray(L0_attn_select_proj_b, f32)
    gate = np.asarray(L0_attn_select_gate, f32)
    Wxp = np.asarray(L0_attn_x_proj_w, f32)
    Wdt = np.asarray(L0_attn_dt_proj_w, f32)
    bdt = np.asarray(L0_attn_dt_proj_b, f32)
    A_log = np.asarray(L0_attn_A_log, f32)
    Dp = np.asarray(L0_attn_D_param, f32)
    Wout = np.asarray(L0_attn_out_proj_w, f32)
    Wg = np.asarray(L0_ffn_w_gate, f32)
    Wu = np.asarray(L0_ffn_w_up, f32)
    Wd = np.asarray(L0_ffn_w_down, f32)

    C = np.ascontiguousarray
    WoutH = C(Wout.T.reshape(NB, 128, NB, 128).transpose(1, 2, 0, 3).reshape(128, -1))
    xT = [C(x[b].T) for b in range(B)]
    XCb = [C(xT[b].reshape(NB, 128, NCC, TCC).transpose(2, 0, 1, 3).reshape(-1, TCC))
           for b in range(B)]
    nw16 = nw.reshape(NB, 128)
    gate16 = gate.reshape(NB, 128)
    bsel16 = bsel.reshape(NB, 128)
    DpC = C(Dp.reshape(NB, 128).T)
    Wsel4 = Wsel.T.reshape(NB, 128, NB, 128)
    Wxp4 = Wxp.T.reshape(NB, 128, 3, 128)
    WgT, WuT, WdT = Wg.T, Wu.T, Wd.T
    ones16 = np.ones((128, 1), BF)
    ones32 = np.ones((128, 1), f32)
    onesr16 = np.ones((1, 128), BF)
    onesr32 = np.ones((1, 128), f32)
    ident16 = np.eye(128, dtype=f32).astype(BF)

    in_maps = []
    for r in range(8):
        b, q = r // 4, r % 4
        perm = [(q * LOC + i) % NB for i in range(NB)]
        XBp = C(xT[b].reshape(NB, 128, NCA, TCA)[perm].transpose(2, 0, 1, 3).reshape(-1, TCA))
        WselH = C(Wsel4[perm][:, :, perm, :].transpose(1, 2, 0, 3).reshape(128, -1))
        WxpH = C(Wxp4[perm].transpose(1, 0, 2, 3).reshape(128, -1))
        WdtH = C(Wdt.T[:, q * 512:(q + 1) * 512])
        WgH = C(WgT[:, q * 2048:(q + 1) * 2048].reshape(NB, 128, HL, 128)
                .transpose(1, 2, 0, 3).reshape(128, -1))
        WuH = C(WuT[:, q * 2048:(q + 1) * 2048].reshape(NB, 128, HL, 128)
                .transpose(1, 2, 0, 3).reshape(128, -1))
        WdH = C(WdT[q * 2048:(q + 1) * 2048, :].reshape(HL, 128, NB, 128)
                .transpose(1, 2, 0, 3).reshape(128, -1))
        in_maps.append(dict(
            x_blk=XBp,
            xc_blk=XCb[b],
            wsel_h=WselH, wxp_h=WxpH, wdt_h=WdtH, wout_h=WoutH,
            wg_h=WgH, wu_h=WuH, wd_h=WdH,
            nw_c=C(nw16[perm].T), gate_c=C(gate16[perm].T), bsel_c=C(bsel16[perm].T),
            dp_c=DpC,
            bdt_c=C(bdt[q * 512:(q + 1) * 512].reshape(LOC, 128).T),
            alog_c=C(A_log[q * 512:(q + 1) * 512].reshape(LOC, 128, 128)
                     .transpose(1, 0, 2).reshape(128, LOC * 128)),
            ones16=ones16, ones32=ones32, onesr16=onesr16, onesr32=onesr32,
            ident16=ident16,
        ))

    nc = build_program()
    res = run_bass_kernel_spmd(nc, in_maps, core_ids=list(range(8)))
    globals()["_last_res"] = res

    out = np.empty((B, L, D), f32)
    for b in range(B):
        for q in range(4):
            oq = res.results[b * 4 + q]["out_q"]
            out[b, q * 1024:(q + 1) * 1024, :] = oq.T
    return out


if __name__ == "__main__":
    import time
    t0 = time.time()
    import sys
    nc = build_program(int(sys.argv[1]) if len(sys.argv) > 1 else 4)
    print(f"build ok: {time.time()-t0:.1f}s, n_instr={nc.n_instructions()}")


# version 18 (final)
# speedup vs baseline: 1.0891x; 1.0891x over previous
import numpy as np
import ml_dtypes
from contextlib import ExitStack
import concourse.bass as bass
import concourse.mybir as mybir
from concourse.bass_utils import run_bass_kernel_spmd

F32 = mybir.dt.float32
F16 = mybir.dt.float16
BF16 = mybir.dt.bfloat16
AF = mybir.ActivationFunctionType
ALU = mybir.AluOpType
BF = ml_dtypes.bfloat16

B, L, D, N, R = 2, 4096, 2048, 128, 128
NB, LOC = 16, 4
TCA, NCA = 512, 8
TCS, NCS = 1024, 4
TCC, NCC = 512, 8
HL = 16


def build_program(phases=4, dbg_y=False):
    nc = bass.Bass("TRN2", target_bir_lowering=False, debug=False)
    ap = lambda name, shape, dt_, kind=None: (
        nc.dram_tensor(name, shape, dt_, kind=kind) if kind else nc.dram_tensor(name, shape, dt_))

    XB = ap("x_blk", [NCA * NB * 128, TCA], F32, "ExternalInput").ap()
    XC = ap("xc_blk", [NCC * NB * 128, TCC], F32, "ExternalInput").ap()
    WSEL = ap("wsel_h", [128, NB * NB * 128], F32, "ExternalInput").ap()
    WXP = ap("wxp_h", [128, NB * 3 * 128], F32, "ExternalInput").ap()
    WDT = ap("wdt_h", [128, LOC * 128], F32, "ExternalInput").ap()
    WOUT = ap("wout_h", [128, NB * NB * 128], F32, "ExternalInput").ap()
    WG = ap("wg_h", [128, HL * NB * 128], F32, "ExternalInput").ap()
    WU = ap("wu_h", [128, HL * NB * 128], F32, "ExternalInput").ap()
    WD = ap("wd_h", [128, NB * HL * 128], F32, "ExternalInput").ap()
    NWC = ap("nw_c", [128, NB], F32, "ExternalInput").ap()
    GTC = ap("gate_c", [128, NB], F32, "ExternalInput").ap()
    BSC = ap("bsel_c", [128, NB], F32, "ExternalInput").ap()
    DPC = ap("dp_c", [128, NB], F32, "ExternalInput").ap()
    BDTC = ap("bdt_c", [128, LOC], F32, "ExternalInput").ap()
    ALC = ap("alog_c", [128, LOC * 128], F32, "ExternalInput").ap()
    ONE16 = ap("ones16", [128, 1], BF16, "ExternalInput").ap()
    ONE32 = ap("ones32", [128, 1], F32, "ExternalInput").ap()
    ONER32 = ap("onesr32", [1, 128], F32, "ExternalInput").ap()
    ONER16 = ap("onesr16", [1, 128], BF16, "ExternalInput").ap()
    IDT = ap("ident16", [128, 128], BF16, "ExternalInput").ap()
    OUTQ = ap("out_q", [NB * 128, 1024], F32, "ExternalOutput").ap()
    DBGY = ap("dbg_y", [D, L], BF16, "ExternalOutput").ap() if dbg_y else None
    DBGD = ap("dbg_dt", [128, LOC * L], F16, "ExternalOutput").ap() if dbg_y else None
    DBGX = ap("dbg_dtx", [128, LOC * L], BF16, "ExternalOutput").ap() if dbg_y else None
    DBGB = ap("dbg_bt", [128, L], BF16, "ExternalOutput").ap() if dbg_y else None
    DBGC = ap("dbg_ct", [128, L], BF16, "ExternalOutput").ap() if dbg_y else None
    DBGA = ap("dbg_ac", [128, LOC * 128], F32, "ExternalOutput").ap() if dbg_y else None
    DBGL = ap("dbg_yl", [128, LOC * L], BF16, "ExternalOutput").ap() if dbg_y else None

    WOS = nc.dram_tensor("wout_s", [128, NB * NB * 128], BF16)
    WGS = nc.dram_tensor("wg_s", [128, HL * NB * 128], BF16)
    WUS = nc.dram_tensor("wu_s", [128, HL * NB * 128], BF16)
    WDS = nc.dram_tensor("wd_s", [128, NB * HL * 128], BF16)
    CCI = nc.dram_tensor("cc_in", [LOC * 128, L], BF16)
    CCO = nc.dram_tensor("cc_out", [D, L], BF16)
    RSI = nc.dram_tensor("rs_in", [4 * D, 1024], BF16)
    RSO = nc.dram_tensor("rs_out", [D, 1024], BF16)

    streams = {k: [] for k in ("sync", "scalar", "vector", "tensor", "gpsimd")}
    cnt = {}
    sems = {}

    def sem(name):
        return sems[name]

    def bump(name, k=1):
        cnt[name] = cnt.get(name, 0) + k
        return cnt[name]

    def val(name):
        return cnt.get(name, 0)

    def emit(engname, fn):
        streams[engname].append(fn)

    big = ExitStack()
    sb = lambda n, s, d: big.enter_context(nc.sbuf_tensor(n, s, d))
    for s_ in ("dw", "dgw", "dx", "drs", "t_rs", "t_bc",
               "bs0", "bs1", "bs2", "bs3", "bs4", "bs5", "cs0", "cs1", "cs2", "cs3", "cs4", "cs5",
               "dx0", "dx1", "dx2", "dx3", "wl0", "wl1",
               "wo0", "wo1", "gu0", "gu1", "dcc", "dyc", "dxc", "dln",
               "drsi", "dr2", "dout", "cc",
               "s_sq", "s_rs", "s_mask", "s_dcb", "s_dt", "s_a", "s_cr",
               "s_yev", "s_sg", "s_row", "s_h116",
               "v_xn", "v_xnm", "v_dtx", "v_u", "v_sc", "v_w", "v_z", "v_r1",
               "v_h1", "v_g", "v_pt", "v_var", "v_o", "v_ini",
               "t_ms", "t_sel", "t_xp", "t_dt", "t_y", "t_mx", "t_ln", "t_gu",
               "t_dn"):
        sems[s_] = big.enter_context(nc.semaphore(s_))

    acols = sb("acols", [128, LOC * 128], F32)
    sstate = sb("sstate", [128, LOC * 128], F32)
    nwc = sb("nwc", [128, NB], F32)
    gtc = sb("gtc", [128, NB], F32)
    bsc = sb("bsc", [128, NB], F32)
    gbc = sb("gbc", [128, NB], F32)
    dpc = sb("dpc", [128, NB], F32)
    bdtc = sb("bdtc", [128, LOC], F32)
    one16 = sb("one16", [128, 1], BF16)
    one32 = sb("one32", [128, 1], F32)
    oner32 = sb("oner32", [1, 128], F32)
    ones16b = sb("ones16b", [1, 128], BF16)
    idt = sb("idt", [128, 128], BF16)
    ps = [big.enter_context(nc.psum_tensor(f"ps{i}", [128, 512], F32)) for i in range(8)]

    stage1 = ExitStack()
    sb1 = lambda n, s, d: stage1.enter_context(nc.sbuf_tensor(n, s, d))
    dtT = sb1("dtT", [128, LOC * L], F16)
    dtx16 = sb1("dtx16", [128, LOC * L], BF16)
    BT = sb1("BT", [128, L], BF16)
    CT = sb1("CT", [128, L], BF16)

    def f(sync):
        for dst, src in ((nwc, NWC), (gtc, GTC), (bsc, BSC), (dpc, DPC),
                         (bdtc, BDTC), (one16, ONE16), (one32, ONE32),
                         (oner32, ONER32), (ones16b, ONER16), (idt, IDT), (acols, ALC)):
            sync.dma_start(out=dst[:], in_=src).then_inc(sem("dw"), 16)
    emit("sync", f)
    bump("dw", 16 * 11)
    dw_tok = val("dw")

    def f(gp):
        gp.dma_start(out=wxp16[:], in_=WXP).then_inc(sem("dgw"), 16)
        gp.dma_start(out=wdt16[:], in_=WDT).then_inc(sem("dgw"), 16)
    init_gp = f
    bump("dgw", 16 * 2)
    dgw_init = val("dgw")

    def f(scalar, dw_tok=dw_tok):
        scalar.wait_ge(sem("dw"), dw_tok)
        scalar.activation(out=sstate[:], in_=acols[:], func=AF.Exp,
                          scale=1.0).then_inc(sem("s_row"), 1)
    emit("scalar", f)
    bump("s_row", 1)
    sr_tok = val("s_row")

    def f(vector, dw_tok=dw_tok, sr_tok=sr_tok):
        vector.wait_ge(sem("dw"), dw_tok)
        vector.tensor_tensor(out=gbc[:], in0=gtc[:], in1=bsc[:], op=ALU.mult)
        vector.wait_ge(sem("s_row"), sr_tok)
        vector.tensor_scalar(out=acols[:], in0=sstate[:], scalar1=-1.0,
                             scalar2=None, op0=ALU.mult)
        vector.memset(sstate[:], 0.0)
        vector.engine_nop().then_inc(sem("v_ini"), 1)
    emit("vector", f)
    bump("v_ini", 1)

    stA = ExitStack()
    sbA = lambda n, s, d: stA.enter_context(nc.sbuf_tensor(n, s, d))
    wselb = [sbA(f"wselb{i}", [128, NB * 128], BF16) for i in range(2)]
    wxp16 = sbA("wxp16", [128, NB * 3 * 128], BF16)
    wdt16 = sbA("wdt16", [128, LOC * 128], BF16)
    emit("gpsimd", init_gp)
    xb = [sbA(f"xa{i}", [128, TCA], F32) for i in range(NB)]
    xn16 = [sbA(f"xn{i}", [128, TCA], BF16) for i in range(NB)]
    xnm16 = [sbA(f"xnm{i}", [128, TCA], BF16) for i in range(NB)]
    xsq = [sbA(f"xsq{i}", [128, TCA], BF16) for i in range(2)]
    msk = [sbA(f"msk{i}", [128, TCA], BF16) for i in range(2)]
    d16 = sbA("delta16", [128, TCA], BF16)
    dtcl = [sbA(f"dtcl{i}", [128, TCA], F32) for i in range(2)]
    dtsg = [sbA(f"dtsg{i}", [128, TCA], F32) for i in range(2)]
    dtsg2 = [sbA(f"dtsg2_{i}", [128, TCA], F32) for i in range(2)]
    rsrow = sbA("rsrow", [1, TCA], F32)
    rstmp = sbA("rstmp", [1, TCA], F32)

    for c in range(NCA):
        xn_prev = val("v_xn")
        def f(sync, c=c, xn_need=(xn_prev if c >= 1 else None)):
            if xn_need is not None:
                sync.wait_ge(sem("v_xn"), xn_need)
            for k in range(NB):
                sync.dma_start(out=xb[k][:], in_=XB[(c * NB + k) * 128:(c * NB + k + 1) * 128, :]
                               ).then_inc(sem(f"dx{k % 4}"), 16)
        emit("sync", f)
        for k in range(NB):
            bump(f"dx{k % 4}", 16)
        dxg_tok = {k: val(f"dx{k % 4}") for k in range(NB)}

        tms0 = val("t_ms")
        for k in range(NB):
            def f(scalar, k=k, dxt=dxg_tok[k], need=(tms0 + k - 1), gate=(k >= 2 or c >= 1)):
                scalar.wait_ge(sem(f"dx{k % 4}"), dxt)
                if gate and need > 0:
                    scalar.wait_ge(sem("t_ms"), need)
                scalar.activation(out=xsq[k % 2][:], in_=xb[k][:],
                                  func=AF.Square).then_inc(sem("s_sq"), 1)
            emit("scalar", f)
            bump("s_sq", 1)
            sq_tok = val("s_sq")
            def f(tensor, k=k, sq_tok=sq_tok, xn_free=(16 * c if c >= 1 else None)):
                tensor.wait_ge(sem("s_sq"), sq_tok)
                if k == 0:
                    tensor.wait_ge(sem("v_ini"), 1)
                    if xn_free is not None:
                        tensor.wait_ge(sem("v_xn"), xn_free)
                tensor.matmul(ps[0][0:1, 0:TCA], one16[:], xsq[k % 2][:],
                              start=(k == 0), stop=(k == NB - 1)).then_inc(sem("t_ms"), 1)
            emit("tensor", f)
            bump("t_ms", 1)
        tms_tok = val("t_ms")
        def f(vector, tms_tok=tms_tok):
            vector.wait_ge(sem("t_ms"), tms_tok)
            vector.tensor_scalar(out=rstmp[:], in0=ps[0][0:1, 0:TCA], scalar1=1.0 / D,
                                 scalar2=1e-6, op0=ALU.mult, op1=ALU.add).then_inc(sem("v_var"), 1)
        emit("vector", f)
        bump("v_var", 1)
        vts_tok = val("v_var")
        def f(scalar, vts_tok=vts_tok, mylog=val("s_rs") + 1):
            scalar.wait_ge(sem("v_var"), vts_tok)
            scalar.activation(out=rsrow[:], in_=rstmp[:], func=AF.Ln,
                              scale=1.0).then_inc(sem("s_rs"), 1)
            scalar.wait_ge(sem("s_rs"), mylog)
            scalar.activation(out=rstmp[:], in_=rsrow[:], func=AF.Exp,
                              scale=-0.5).then_inc(sem("s_rs"), 1)
        emit("scalar", f)
        bump("s_rs", 2)
        vrs_tok = val("s_rs")
        def f(tensor, rs_tok=vrs_tok, vts_tok=vts_tok):
            tensor.wait_ge(sem("s_rs"), rs_tok)
            tensor.wait_ge(sem("v_var"), vts_tok)
            tensor.matmul(ps[0][:, 0:TCA], oner32[:], rstmp[0:1, :],
                          start=True, stop=True).then_inc(sem("t_rs"), 1)
        emit("tensor", f)
        bump("t_rs", 1)
        drs_tok = val("t_rs")

        tsel_prev = val("t_sel")
        for k in range(NB):
            def f(vector, k=k, drs_tok=drs_tok, c=c,
                  tsel_need=(tsel_prev if c >= 1 else None)):
                if k == 0:
                    vector.wait_ge(sem("t_rs"), drs_tok)
                    if tsel_need is not None:
                        vector.wait_ge(sem("t_sel"), tsel_need)
                vector.scalar_tensor_tensor(out=xn16[k][:], in0=xb[k][:],
                                            scalar=nwc[:, k:k + 1], in1=ps[0][:, 0:TCA],
                                            op0=ALU.mult, op1=ALU.mult).then_inc(sem("v_xn"), 1)
            emit("vector", f)
            bump("v_xn", 1)
        vxn_tok = val("v_xn")

        for do in range(NB):
            smask0 = val("s_mask")
            wsem = "wl0" if do % 2 == 0 else "wl1"
            def fw(gp, do=do, wsem=wsem, c=c,
                   tsel_need=(val("t_sel") - 1 if (do >= 2 or c >= 1) else None)):
                if tsel_need is not None and tsel_need > 0:
                    gp.wait_ge(sem("t_sel"), tsel_need)
                gp.dma_start(out=wselb[do % 2][:],
                             in_=WSEL[:, do * 2048:(do + 1) * 2048]).then_inc(sem(wsem), 16)
            emit("gpsimd", fw)
            bump(wsem, 16)
            wl_tok = val(wsem)
            def f(tensor, do=do, vxn_tok=vxn_tok, need=(smask0 - 1), wsem=wsem,
                  wl_tok=wl_tok, gate=(do >= 2 or c >= 1)):
                if do == 0:
                    tensor.wait_ge(sem("v_xn"), vxn_tok)
                tensor.wait_ge(sem(wsem), wl_tok)
                if gate and need > 0:
                    tensor.wait_ge(sem("s_mask"), need)
                for di in range(NB):
                    mm = tensor.matmul(ps[1 + do % 2][:, 0:TCA],
                                       wselb[do % 2][:, di * 128:(di + 1) * 128],
                                       xn16[di][:], start=(di == 0), stop=(di == NB - 1))
                    if di == NB - 1:
                        mm.then_inc(sem("t_sel"), 1)
            emit("tensor", f)
            bump("t_sel", 1)
            tsel_tok = val("t_sel")
            def f(scalar, do=do, tsel_tok=tsel_tok):
                scalar.wait_ge(sem("t_sel"), tsel_tok)
                scalar.activation(out=msk[do % 2][:], in_=ps[1 + do % 2][:, 0:TCA],
                                  func=AF.Sigmoid, scale=gtc[:, do:do + 1],
                                  bias=gbc[:, do:do + 1]).then_inc(sem("s_mask"), 1)
            emit("scalar", f)
            bump("s_mask", 1)
            sm_tok = val("s_mask")
            def f(vector, do=do, sm_tok=sm_tok):
                vector.wait_ge(sem("s_mask"), sm_tok)
                vector.tensor_tensor(out=xnm16[do][:], in0=xn16[do][:],
                                     in1=msk[do % 2][:], op=ALU.mult).then_inc(sem("v_xnm"), 1)
            emit("vector", f)
            bump("v_xnm", 1)
        vxnm_tok = val("v_xnm")

        for ob in range(3):
            def f(tensor, ob=ob, vxnm_tok=vxnm_tok, sd_need=(val("s_dcb") - 3 if c >= 1 else None)):
                if ob == 0:
                    tensor.wait_ge(sem("v_xnm"), vxnm_tok)
                    if sd_need is not None:
                        tensor.wait_ge(sem("s_dcb"), sd_need + 3)
                for di in range(NB):
                    mm = tensor.matmul(ps[3 + ob][:, 0:TCA],
                                       wxp16[:, (di * 3 + ob) * 128:(di * 3 + ob + 1) * 128],
                                       xnm16[di][:], start=(di == 0), stop=(di == NB - 1))
                    if di == NB - 1:
                        mm.then_inc(sem("t_xp"), 1)
            emit("tensor", f)
            bump("t_xp", 1)
        txp_tok = val("t_xp")
        def f(scalar, c=c, txp_tok=txp_tok):
            scalar.wait_ge(sem("t_xp"), txp_tok)
            scalar.copy(out=d16[:], in_=ps[3][:, 0:TCA]).then_inc(sem("s_dcb"), 1)
            scalar.copy(out=BT[:, c * TCA:(c + 1) * TCA], in_=ps[4][:, 0:TCA]).then_inc(sem("s_dcb"), 1)
            scalar.copy(out=CT[:, c * TCA:(c + 1) * TCA], in_=ps[5][:, 0:TCA]).then_inc(sem("s_dcb"), 1)
        emit("scalar", f)
        bump("s_dcb", 3)
        sdcb_tok = val("s_dcb")

        for pair in range(2):
            for j in range(2):
                k2 = 2 * pair + j
                def f(tensor, k2=k2, sdcb_tok=sdcb_tok, vcl_need=(val("v_dtx") if c + k2 else None)):
                    if k2 == 0:
                        tensor.wait_ge(sem("s_dcb"), sdcb_tok - 2)
                    if vcl_need is not None:
                        tensor.wait_ge(sem("v_dtx"), vcl_need)
                    tensor.matmul(ps[6 + k2 % 2][:, 0:TCA], wdt16[:, k2 * 128:(k2 + 1) * 128],
                                  d16[:], start=True, stop=True).then_inc(sem("t_dt"), 1)
                emit("tensor", f)
                bump("t_dt", 1)
                tdt_tok = val("t_dt")
                def f(vector, k2=k2, j=j, tdt_tok=tdt_tok):
                    vector.wait_ge(sem("t_dt"), tdt_tok)
                    vector.tensor_scalar(out=dtcl[j][:], in0=ps[6 + k2 % 2][:, 0:TCA],
                                         scalar1=bdtc[:, k2:k2 + 1], scalar2=-80.0,
                                         op0=ALU.add, op1=ALU.max).then_inc(sem("v_dtx"), 1)
                emit("vector", f)
                bump("v_dtx", 1)
            vcl_tok = val("v_dtx")
            def f(scalar, vcl_tok=vcl_tok):
                scalar.wait_ge(sem("v_dtx"), vcl_tok)
                for j in range(2):
                    scalar.activation(out=dtsg2[j][:], in_=dtcl[j][:],
                                      func=AF.Sigmoid, scale=1.0).then_inc(sem("s_dt"), 1)
                for j in range(2):
                    scalar.activation(out=dtsg[j][:], in_=dtsg2[j][:],
                                      func=AF.Ln, scale=1.0).then_inc(sem("s_dt"), 1)
            emit("scalar", f)
            bump("s_dt", 4)
            sdt_tok = val("s_dt")
            for j in range(2):
                k2 = 2 * pair + j
                def f(vector, k2=k2, j=j, c=c, sdt_tok=sdt_tok, mysub=val("v_dtx") + 1):
                    if j == 0:
                        vector.wait_ge(sem("s_dt"), sdt_tok)
                    vector.tensor_tensor(out=dtT[:, k2 * L + c * TCA: k2 * L + (c + 1) * TCA],
                                         in0=dtcl[j][:], in1=dtsg[j][:],
                                         op=ALU.subtract).then_inc(sem("v_dtx"), 1)
                    vector.wait_ge(sem("v_dtx"), mysub)
                    vector.tensor_tensor(out=dtx16[:, k2 * L + c * TCA: k2 * L + (c + 1) * TCA],
                                         in0=dtT[:, k2 * L + c * TCA: k2 * L + (c + 1) * TCA],
                                         in1=xnm16[k2][:], op=ALU.mult).then_inc(sem("v_dtx"), 1)
                emit("vector", f)
                bump("v_dtx", 2)

    a_vdtx = val("v_dtx")
    a_sdcb = val("s_dcb")
    ckpt1 = {k: len(v) for k, v in streams.items()}

    def f(gp):
        PIECES = 4
        for src, dst in ((WOUT, WOS), (WG, WGS), (WU, WUS), (WD, WDS)):
            tot = 32768
            step = tot // PIECES
            for i in range(PIECES):
                gp.dma_start(out=dst.ap()[:, i * step:(i + 1) * step],
                             in_=src[:, i * step:(i + 1) * step]).then_inc(sem("dgw"), 16)
    emit("gpsimd", f)
    bump("dgw", 16 * 16)
    dgw_cast = val("dgw")

    stA.close()

    stS = ExitStack()
    sbS = lambda n, s, d: stS.enter_context(nc.sbuf_tensor(n, s, d))
    sa = [[sbS(f"sa{i}_{k}", [128, TCS], BF16) for k in range(LOC)] for i in range(2)]
    su = [sbS(f"su{i}", [128, TCS], BF16) for i in range(LOC)]
    sh = [[sbS(f"sh{i}_{k}", [128, TCS], BF16) for k in range(LOC)] for i in range(2)]
    sw = [[sbS(f"sw{i}_{k}", [128, TCS], BF16) for k in range(LOC)] for i in range(2)]
    brepb = [sbS(f"brepb{i}", [128, TCS], BF16) for i in range(6)]
    crepb = [sbS(f"crepb{i}", [128, TCS], BF16) for i in range(6)]
    y16 = sbS("y16", [128, LOC * L], BF16)

    vu_t, vsc_t, vw_t, cr_t, ty_t = {}, {}, {}, {}, {}
    NIT = NCS * 128
    BSEMS = ["bs0", "bs1", "bs2", "bs3", "bs4", "bs5"]
    CSEMS = ["cs0", "cs1", "cs2", "cs3", "cs4", "cs5"]
    bs_tok, cs_tok = {}, {}

    def emit_bcast(jt, first=False):
        cc_, nn_ = jt // 128, jt % 128
        slot = jt % 6
        def fb(sync, jt=jt, cc_=cc_, nn_=nn_, slot=slot, first=first,
               vu_need=(vu_t.get(jt - 6))):
            if first:
                sync.wait_ge(sem("s_dcb"), a_sdcb)
            if vu_need is not None:
                sync.wait_ge(sem("v_u"), vu_need)
            srcb = BT[nn_:nn_ + 1, cc_ * TCS:(cc_ + 1) * TCS].unsqueeze(1).broadcast_to([1, 128, TCS])
            sync.dma_start(out=brepb[slot][:], in_=srcb).then_inc(sem(BSEMS[slot]), 16)
        emit("sync", fb)
        bump(BSEMS[slot], 16)
        bs_tok[jt] = val(BSEMS[slot])
        def fc(gp, jt=jt, cc_=cc_, nn_=nn_, slot=slot, first=first,
               vw_need=(vw_t.get(jt - 6))):
            if first:
                gp.wait_ge(sem("s_dcb"), a_sdcb)
            if vw_need is not None:
                gp.wait_ge(sem("v_w"), vw_need)
            srcc = CT[nn_:nn_ + 1, cc_ * TCS:(cc_ + 1) * TCS].unsqueeze(1).broadcast_to([1, 128, TCS])
            gp.dma_start(out=crepb[slot][:], in_=srcc).then_inc(sem(CSEMS[slot]), 16)
        emit("gpsimd", fc)
        bump(CSEMS[slot], 16)
        cs_tok[jt] = val(CSEMS[slot])

    it = 0
    for c in range(NCS):
        for n in range(128):
            buf = it % 2
            slot = it % 6
            if it == 0:
                for j in range(6):
                    emit_bcast(j, first=(j == 0))

            vsc_need = vsc_t.get((it - 2))
            for k in range(LOC):
                def f(scalar, k=k, n=n, c=c, buf=buf, first=(it == 0),
                      vsc_need=(vsc_need if it >= 2 else None)):
                    if first and k == 0:
                        scalar.wait_ge(sem("v_dtx"), a_vdtx)
                    if k == 0 and vsc_need is not None:
                        scalar.wait_ge(sem("v_sc"), vsc_need)
                    scalar.activation(out=sa[buf][k][:],
                                      in_=dtT[:, k * L + c * TCS: k * L + (c + 1) * TCS],
                                      func=AF.Exp,
                                      scale=acols[:, k * 128 + n:k * 128 + n + 1]
                                      ).then_inc(sem("s_a"), 1)
                emit("scalar", f)
                bump("s_a", 1)
            sa_tok = val("s_a")

            utoks = {}
            for k in range(LOC):
                bump("v_u", 1)
                utoks[k] = val("v_u")
                vu_t[it] = val("v_u")
            def f(vector, n=n, c=c, buf=buf, slot=slot, sa_tok=sa_tok,
                  bst=bs_tok[it], cst=cs_tok[it],
                  ty_need=(ty_t.get((it - 2)) if it >= 2 else None)):
                vector.wait_ge(sem("s_a"), sa_tok)
                vector.wait_ge(sem(BSEMS[slot]), bst)
                vector.wait_ge(sem(CSEMS[slot]), cst)
                if ty_need is not None:
                    vector.wait_ge(sem("t_y"), ty_need)
                for k in range(LOC):
                    vector.tensor_tensor(out=su[k][:],
                                         in0=dtx16[:, k * L + c * TCS: k * L + (c + 1) * TCS],
                                         in1=brepb[slot][:], op=ALU.mult).then_inc(sem("v_u"), 1)
            emit("vector", f)
            sctoks = {}
            for k in range(LOC):
                bump("v_sc", 1)
                sctoks[k] = val("v_sc")
                vsc_t[it] = val("v_sc")
            crns, shbs = {}, {}
            for k in range(LOC):
                crns[k] = cr_t.get((c - 1, n, k)) if c > 0 else None
                pten = it - 2
                shbs[k] = None
                if pten >= 0 and (pten // 128) < NCS - 1:
                    shbs[k] = cr_t.get((pten // 128, pten % 128, k))
            def f(vector, n=n, c=c, buf=buf, utoks=dict(utoks), crns=dict(crns),
                  shbs=dict(shbs)):
                for k in range(LOC):
                    vector.wait_ge(sem("v_u"), utoks[k])
                    if crns[k] is not None:
                        vector.wait_ge(sem("s_cr"), crns[k])
                    if shbs[k] is not None:
                        vector.wait_ge(sem("s_cr"), shbs[k])
                    col = k * 128 + n
                    vector.tensor_tensor_scan(out=sh[buf][k][:], data0=sa[buf][k][:],
                                              data1=su[k][:],
                                              initial=sstate[:, col:col + 1],
                                              op0=ALU.mult, op1=ALU.add).then_inc(sem("v_sc"), 1)
            emit("vector", f)
            wtoks = {}
            for k in range(LOC):
                bump("v_w", 1)
                wtoks[k] = val("v_w")
                vw_t[it] = val("v_w")
            def f(vector, buf=buf, slot=slot, sctoks=dict(sctoks)):
                for k in range(LOC):
                    vector.wait_ge(sem("v_sc"), sctoks[k])
                    vector.tensor_tensor(out=sw[buf][k][:], in0=sh[buf][k][:],
                                         in1=crepb[slot][:], op=ALU.mult).then_inc(sem("v_w"), 1)
            emit("vector", f)
            for k in range(LOC):
                if c < NCS - 1:
                    def f(gp, k=k, buf=buf, col=k * 128 + n, vst=sctoks[k]):
                        gp.wait_ge(sem("v_sc"), vst)
                        gp.tensor_copy(out=sstate[:, col:col + 1],
                                       in_=sh[buf][k][:, TCS - 1:TCS]).then_inc(sem("s_cr"), 1)
                    emit("gpsimd", f)
                    bump("s_cr", 1)
                    cr_t[(c, n, k)] = val("s_cr")
                def f(tensor, k=k, n=n, buf=buf, vwk=wtoks[k],
                      yev_need=(val("s_yev") if (n == 0 and c > 0 and k == 0) else None)):
                    tensor.wait_ge(sem("v_w"), vwk)
                    if yev_need is not None:
                        tensor.wait_ge(sem("s_yev"), yev_need)
                    for s in range(2):
                        mm = tensor.matmul(ps[2 * k + s][:], idt[:],
                                           sw[buf][k][:, s * 512:(s + 1) * 512],
                                           start=(n == 0), stop=(n == 127))
                        if s == 1:
                            mm.then_inc(sem("t_y"), 1)
                emit("tensor", f)
                bump("t_y", 1)
            ty_t[it] = val("t_y")
            if it + 6 < NIT:
                emit_bcast(it + 6)
            it += 1

        ty_tok = val("t_y")
        def f(scalar, c=c, ty_tok=ty_tok):
            scalar.wait_ge(sem("t_y"), ty_tok)
            for k in range(LOC):
                for s in range(2):
                    scalar.copy(out=y16[:, k * L + c * TCS + s * 512: k * L + c * TCS + (s + 1) * 512],
                                in_=ps[2 * k + s][:]).then_inc(sem("s_yev"), 1)
        emit("scalar", f)
        bump("s_yev", 2 * LOC)

    yev_all = val("s_yev")
    def f(gp, yev_all=yev_all):
        gp.wait_ge(sem("s_yev"), yev_all)
        if DBGY is not None:
            gp.dma_start(out=DBGD, in_=dtT[:]).then_inc(sem("dcc"), 16)
            gp.dma_start(out=DBGX, in_=dtx16[:]).then_inc(sem("dcc"), 16)
            gp.dma_start(out=DBGB, in_=BT[:]).then_inc(sem("dcc"), 16)
            gp.dma_start(out=DBGC, in_=CT[:]).then_inc(sem("dcc"), 16)
            gp.dma_start(out=DBGA, in_=acols[:]).then_inc(sem("dcc"), 16)
            gp.dma_start(out=DBGL, in_=y16[:]).then_inc(sem("dcc"), 16)
            gp.wait_ge(sem("dcc"), 96)
        base_dcc = 96 if DBGY is not None else 0
        for k in range(LOC):
            gp.dma_start(out=CCI.ap()[k * 128:(k + 1) * 128, :],
                         in_=y16[:, k * L:(k + 1) * L]).then_inc(sem("dcc"), 16)
        gp.wait_ge(sem("dcc"), base_dcc + 16 * LOC)
        gp.collective_compute("AllGather", ALU.bypass,
                              replica_groups=[[0, 1, 2, 3], [4, 5, 6, 7]],
                              ins=[CCI.ap().opt()],
                              outs=[CCO.ap().opt()]).then_inc(sem("cc"), 1)
        gp.wait_ge(sem("cc"), 1)
        if DBGY is not None:
            gp.dma_start(out=DBGY, in_=CCO.ap()).then_inc(sem("dcc"), 16)
    emit("gpsimd", f)
    bump("dcc", 16 * LOC)
    bump("cc", 1)

    ckpt3 = {k: len(v) for k, v in streams.items()}
    stS.close()
    stage1.close()

    stC = ExitStack()
    sbC = lambda n, s, d: stC.enter_context(nc.sbuf_tensor(n, s, d))
    xc = [sbC(f"xc{i}", [128, TCC], F32) for i in range(NB)]
    yc = [sbC(f"yc{i}", [128, TCC], BF16) for i in range(NB)]
    z16 = [sbC(f"z{i}", [128, TCC], BF16) for i in range(NB)]
    r1 = [sbC(f"r1_{i}", [128, TCC], F32) for i in range(NB)]
    h1 = [sbC(f"h1_{i}", [128, TCC], F32) for i in range(NB)]
    h116 = [sbC(f"h116_{i}", [128, TCC], BF16) for i in range(NB)]
    gated = [sbC(f"gat{i}", [128, TCC], BF16) for i in range(HL)]
    wob = [sbC(f"wob{i}", [128, NB * 128], BF16) for i in range(2)]
    wgb = [sbC(f"wgb{i}", [128, NB * 128], BF16) for i in range(2)]
    wub = [sbC(f"wub{i}", [128, NB * 128], BF16) for i in range(2)]
    sg16 = [sbC("sg0", [128, TCC], BF16)]
    r1sq = [sbC("rsq0", [128, TCC], BF16)]
    tmp32 = sbC("tmp32", [128, TCC], F32)
    mrow = sbC("mrow", [1, TCC], F32)
    msq = sbC("msq", [1, TCC], F32)
    s2row = sbC("s2row", [1, TCC], F32)
    varrow = sbC("varrow", [1, TCC], F32)
    rstdrow = sbC("rstdrow", [1, TCC], F32)
    mrep = sbC("mrep", [128, TCC], F32)
    rstdrep = sbC("rstdrep", [128, TCC], F32)

    for t in range(NCC):
        quarter, half = t // 2, t % 2
        def f(sync, t=t, vr1_need=(val("v_r1") if t >= 1 else None),
              vz_need=(val("v_z") if t >= 1 else None),
              drsi_need=(val("drsi") if t >= 1 else None)):
            if t == 0:
                sync.wait_ge(sem("cc"), 1)
            if vr1_need is not None:
                sync.wait_ge(sem("v_r1"), vr1_need)
            if drsi_need is not None:
                sync.wait_ge(sem("dout" if False else "drsi"), drsi_need)
            for k in range(NB):
                sync.dma_start(out=xc[k][:], in_=XC[(t * NB + k) * 128:(t * NB + k + 1) * 128, :]
                               ).then_inc(sem("dxc"), 16)
            if vz_need is not None:
                sync.wait_ge(sem("v_z"), vz_need)
            for k in range(NB):
                sync.dma_start(out=yc[k][:], in_=CCO.ap()[k * 128:(k + 1) * 128, t * TCC:(t + 1) * TCC]
                               ).then_inc(sem("dyc"), 16)
        emit("sync", f)
        bump("dxc", 16 * NB)
        bump("dyc", 16 * NB)
        dxc_tok, dyc_tok = val("dxc"), val("dyc")

        for do in range(NB):
            def f(gp, do=do, t=t, first=(t == 0 and do == 0),
                  tmx_need=(val("t_mx") - 1 if (t * NB + do) >= 2 else None)):
                if first:
                    gp.wait_ge(sem("dgw"), dgw_cast)
                if tmx_need is not None:
                    gp.wait_ge(sem("t_mx"), tmx_need)
                gp.dma_start(out=wob[do % 2][:], in_=WOS.ap()[:, do * 2048:(do + 1) * 2048]
                             ).then_inc(sem("wo0" if do % 2 == 0 else "wo1"), 16)
            emit("gpsimd", f)
            wsem = "wo0" if do % 2 == 0 else "wo1"
            bump(wsem, 16)
            dgw_tok = val(wsem)

            if do < NB:
                pass
            if do == 0:
                for k in range(NB):
                    def f(vector, k=k, dxc_tok=dxc_tok, dyc_tok=dyc_tok):
                        if k == 0:
                            vector.wait_ge(sem("dxc"), dxc_tok)
                            vector.wait_ge(sem("dyc"), dyc_tok)
                        vector.scalar_tensor_tensor(out=z16[k][:], in0=xc[k][:],
                                                    scalar=dpc[:, k:k + 1], in1=yc[k][:],
                                                    op0=ALU.mult, op1=ALU.add).then_inc(sem("v_z"), 1)
                    emit("vector", f)
                    bump("v_z", 1)
            vz_tok = val("v_z")

            def f(tensor, do=do, vz_tok=vz_tok, dgw_tok=dgw_tok, wsem=wsem,
                  vr1_need=(val("v_r1") - 1 if do >= 2 else None),
                  vpt_need=(val("v_pt") if (do < 2 and t >= 1) else None)):
                if do == 0:
                    tensor.wait_ge(sem("v_z"), vz_tok)
                tensor.wait_ge(sem(wsem), dgw_tok)
                if vr1_need is not None:
                    tensor.wait_ge(sem("v_r1"), vr1_need)
                if vpt_need is not None:
                    tensor.wait_ge(sem("v_pt"), vpt_need)
                for di in range(NB):
                    mm = tensor.matmul(ps[do % 2][:], wob[do % 2][:, di * 128:(di + 1) * 128],
                                       z16[di][:], start=(di == 0), stop=(di == NB - 1))
                    if di == NB - 1:
                        mm.then_inc(sem("t_mx"), 1)
            emit("tensor", f)
            bump("t_mx", 1)
            tmx_tok = val("t_mx")
            def f(vector, do=do, tmx_tok=tmx_tok,
                  drsi_need=(val("drsi") if (do == 0 and t >= 1) else None)):
                if drsi_need is not None:
                    vector.wait_ge(sem("drsi"), drsi_need)
                vector.wait_ge(sem("t_mx"), tmx_tok)
                vector.tensor_tensor(out=r1[do][:], in0=ps[do % 2][:], in1=xc[do][:],
                                     op=ALU.add).then_inc(sem("v_r1"), 1)
            emit("vector", f)
            bump("v_r1", 1)

        vr1_all = val("v_r1")
        for k in range(NB):
            def f(scalar, k=k, vr1_all=vr1_all, tln_need=(val("t_ln") if (k >= 1 or t >= 1) else None)):
                if k == 0:
                    scalar.wait_ge(sem("v_r1"), vr1_all)
                if tln_need is not None:
                    scalar.wait_ge(sem("t_ln"), max(0, tln_need))
                scalar.activation(out=r1sq[0][:], in_=r1[k][:],
                                  func=AF.Square).then_inc(sem("s_sq"), 1)
            emit("scalar", f)
            bump("s_sq", 1)
            ssq_tok = val("s_sq")
            def f(tensor, k=k, vr1_all=vr1_all, ssq_tok=ssq_tok,
                  srow_need=(val("s_row") if (k == 0 and t >= 1) else None)):
                if k == 0:
                    tensor.wait_ge(sem("v_r1"), vr1_all)
                    if srow_need is not None:
                        tensor.wait_ge(sem("s_row"), srow_need)
                tensor.matmul(ps[2][0:1, :], one32[:], r1[k][:],
                              start=(k == 0), stop=(k == NB - 1))
                tensor.wait_ge(sem("s_sq"), ssq_tok)
                mm = tensor.matmul(ps[3][0:1, :], one16[:], r1sq[0][:],
                                   start=(k == 0), stop=(k == NB - 1))
                mm.then_inc(sem("t_ln"), 1)
            emit("tensor", f)
            bump("t_ln", 1)
        tln_tok = val("t_ln")
        def f(scalar, tln_tok=tln_tok):
            scalar.wait_ge(sem("t_ln"), tln_tok)
            scalar.activation(out=mrow[:], in_=ps[2][0:1, :], func=AF.Copy, scale=1.0 / D)
            scalar.activation(out=s2row[:], in_=ps[3][0:1, :], func=AF.Copy, scale=1.0 / D)
            scalar.activation(out=msq[:], in_=mrow[:], func=AF.Square).then_inc(sem("s_row"), 1)
        emit("scalar", f)
        bump("s_row", 1)
        srow_tok = val("s_row")
        def f(vector, srow_tok=srow_tok, myv=val("v_var") + 1):
            vector.wait_ge(sem("s_row"), srow_tok)
            vector.tensor_tensor(out=varrow[:], in0=s2row[:], in1=msq[:],
                                 op=ALU.subtract).then_inc(sem("v_var"), 1)
            vector.wait_ge(sem("v_var"), myv)
            vector.tensor_scalar(out=msq[:], in0=varrow[:], scalar1=1e-5,
                                 scalar2=None, op0=ALU.add).then_inc(sem("v_var"), 1)
        emit("vector", f)
        bump("v_var", 2)
        vvar_tok = val("v_var")
        def f(scalar, vvar_tok=vvar_tok, mylog=val("s_row") + 1):
            scalar.wait_ge(sem("v_var"), vvar_tok)
            scalar.activation(out=varrow[:], in_=msq[:], func=AF.Ln,
                              scale=1.0).then_inc(sem("s_row"), 1)
            scalar.wait_ge(sem("s_row"), mylog)
            scalar.activation(out=rstdrow[:], in_=varrow[:], func=AF.Exp,
                              scale=-0.5).then_inc(sem("s_row"), 1)
        emit("scalar", f)
        bump("s_row", 2)
        vvar2_tok = val("s_row")
        def f(sync, srow_tok=vvar2_tok):
            sync.wait_ge(sem("s_row"), srow_tok)
            sync.dma_start(out=mrep[:], in_=mrow[0:1, :].unsqueeze(1).broadcast_to([1, 128, TCC])
                           ).then_inc(sem("dln"), 16)
            sync.dma_start(out=rstdrep[:], in_=rstdrow[0:1, :].unsqueeze(1).broadcast_to([1, 128, TCC])
                           ).then_inc(sem("dln"), 16)
        emit("sync", f)
        bump("dln", 32)
        dln_tok = val("dln")

        for k in range(NB):
            bump("v_h1", 1)
            myh = val("v_h1")
            def f(vector, k=k, dln_tok=dln_tok, myh=myh):
                if k == 0:
                    vector.wait_ge(sem("dln"), dln_tok)
                vector.tensor_tensor(out=tmp32[:], in0=r1[k][:], in1=mrep[:],
                                     op=ALU.subtract).then_inc(sem("v_h1"), 1)
                vector.wait_ge(sem("v_h1"), myh)
                vector.tensor_tensor(out=h1[k][:], in0=tmp32[:], in1=rstdrep[:],
                                     op=ALU.mult).then_inc(sem("v_h1"), 1)
            emit("vector", f)
            bump("v_h1", 1)
            vh_tok = val("v_h1")
            def f(scalar, k=k, vh_tok=vh_tok):
                scalar.wait_ge(sem("v_h1"), vh_tok)
                scalar.copy(out=h116[k][:], in_=h1[k][:]).then_inc(sem("s_h116"), 1)
            emit("scalar", f)
            bump("s_h116", 1)
        sh116_tok = val("s_h116")

        for hl in range(HL):
            def f(gp, hl=hl, t=t, tgu_need=(val("t_gu") - 2 if (t * HL + hl) >= 2 else None),
                  tdn_need=(val("t_dn") if (t >= 1 and hl < 2) else None)):
                if tgu_need is not None:
                    gp.wait_ge(sem("t_gu"), max(0, tgu_need))
                if tdn_need is not None:
                    gp.wait_ge(sem("t_dn"), tdn_need)
                gsem_ = "gu0" if hl % 2 == 0 else "gu1"
                gp.dma_start(out=wgb[hl % 2][:], in_=WGS.ap()[:, hl * 2048:(hl + 1) * 2048]
                             ).then_inc(sem(gsem_), 16)
            emit("gpsimd", f)
            def f2(eng, hl=hl, t=t, tgu_need=(val("t_gu") - 2 if (t * HL + hl) >= 2 else None)):
                if tgu_need is not None:
                    eng.wait_ge(sem("t_gu"), max(0, tgu_need))
                gsem_ = "gu0" if hl % 2 == 0 else "gu1"
                eng.dma_start(out=wub[hl % 2][:], in_=WUS.ap()[:, hl * 2048:(hl + 1) * 2048]
                              ).then_inc(sem(gsem_), 16)
            emit("sync", f2)
            gsem = "gu0" if hl % 2 == 0 else "gu1"
            bump(gsem, 32)
            dgw_tok = val(gsem)
            def f(tensor, hl=hl, dgw_tok=dgw_tok, gsem=gsem, sh116_tok=sh116_tok,
                  ssg_need=(val("s_sg") - 1 if (hl >= 2 or t >= 1) else None),
                  vg_need=(val("v_g") - 1 if (hl >= 2 or t >= 1) else None)):
                if hl == 0:
                    tensor.wait_ge(sem("s_h116"), sh116_tok)
                tensor.wait_ge(sem(gsem), dgw_tok)
                if ssg_need is not None:
                    tensor.wait_ge(sem("s_sg"), max(0, ssg_need))
                for di in range(NB):
                    mm = tensor.matmul(ps[4 + hl % 2][:], wgb[hl % 2][:, di * 128:(di + 1) * 128],
                                       h116[di][:], start=(di == 0), stop=(di == NB - 1))
                    if di == NB - 1:
                        mm.then_inc(sem("t_gu"), 1)
                if vg_need is not None:
                    tensor.wait_ge(sem("v_g"), max(0, vg_need))
                for di in range(NB):
                    mm = tensor.matmul(ps[6 + hl % 2][:], wub[hl % 2][:, di * 128:(di + 1) * 128],
                                       h116[di][:], start=(di == 0), stop=(di == NB - 1))
                    if di == NB - 1:
                        mm.then_inc(sem("t_gu"), 1)
            emit("tensor", f)
            bump("t_gu", 2)
            tgu_tok = val("t_gu")
            def f(scalar, hl=hl, tgu_tok=tgu_tok, vg_prev=(val("v_g") if (hl >= 1 or t >= 1) else None)):
                scalar.wait_ge(sem("t_gu"), tgu_tok - 1)
                if vg_prev is not None:
                    scalar.wait_ge(sem("v_g"), vg_prev)
                scalar.activation(out=sg16[0][:], in_=ps[4 + hl % 2][:],
                                  func=AF.Silu).then_inc(sem("s_sg"), 1)
            emit("scalar", f)
            bump("s_sg", 1)
            ssg_tok = val("s_sg")
            def f(vector, hl=hl, ssg_tok=ssg_tok, tgu_tok=tgu_tok):
                vector.wait_ge(sem("s_sg"), ssg_tok)
                vector.wait_ge(sem("t_gu"), tgu_tok)
                vector.tensor_tensor(out=gated[hl][:], in0=sg16[0][:],
                                     in1=ps[6 + hl % 2][:], op=ALU.mult).then_inc(sem("v_g"), 1)
            emit("vector", f)
            bump("v_g", 1)
        vg_all = val("v_g")

        tgu_all = val("t_gu")
        for do in range(NB):
            def f(gp, do=do, tdn_need=(val("t_dn") - 1 if do >= 2 else None), tgu_all=tgu_all):
                if do < 2:
                    gp.wait_ge(sem("t_gu"), tgu_all)
                if tdn_need is not None:
                    gp.wait_ge(sem("t_dn"), max(0, tdn_need))
                gsem_ = "gu0" if do % 2 == 0 else "gu1"
                gp.dma_start(out=wgb[do % 2][:], in_=WDS.ap()[:, do * 2048:(do + 1) * 2048]
                             ).then_inc(sem(gsem_), 16)
            emit("gpsimd", f)
            gsem = "gu0" if do % 2 == 0 else "gu1"
            bump(gsem, 16)
            dgw_tok = val(gsem)
            def f(tensor, do=do, vg_all=vg_all, dgw_tok=dgw_tok, gsem=gsem,
                  vpt_need=(val("v_pt") - 1 if do >= 2 else None)):
                if do == 0:
                    tensor.wait_ge(sem("v_g"), vg_all)
                tensor.wait_ge(sem(gsem), dgw_tok)
                if vpt_need is not None:
                    tensor.wait_ge(sem("v_pt"), max(0, vpt_need))
                for hl in range(HL):
                    mm = tensor.matmul(ps[do % 2][:], wgb[do % 2][:, hl * 128:(hl + 1) * 128],
                                       gated[hl][:], start=(hl == 0), stop=(hl == HL - 1))
                    if hl == HL - 1:
                        mm.then_inc(sem("t_dn"), 1)
            emit("tensor", f)
            bump("t_dn", 1)
            tdn_tok = val("t_dn")
            def f(vector, do=do, tdn_tok=tdn_tok):
                vector.wait_ge(sem("t_dn"), tdn_tok)
                vector.scalar_tensor_tensor(out=z16[do][:], in0=h1[do][:], scalar=0.25,
                                            in1=ps[do % 2][:], op0=ALU.mult,
                                            op1=ALU.add).then_inc(sem("v_pt"), 1)
            emit("vector", f)
            bump("v_pt", 1)
            vpt_tok = val("v_pt")
            def f(sync, do=do, t=t, quarter=quarter, half=half, vpt_tok=vpt_tok):
                sync.wait_ge(sem("v_pt"), vpt_tok)
                sync.dma_start(out=RSI.ap()[quarter * D + do * 128:quarter * D + (do + 1) * 128,
                                            half * TCC:(half + 1) * TCC],
                               in_=z16[do][:]).then_inc(sem("drsi"), 16)
            emit("sync", f)
            bump("drsi", 16)

    drsi_all = val("drsi")
    def f(gp, drsi_all=drsi_all):
        gp.wait_ge(sem("drsi"), drsi_all)
        gp.collective_compute("ReduceScatter", ALU.add,
                              replica_groups=[[0, 1, 2, 3], [4, 5, 6, 7]],
                              ins=[RSI.ap().opt()],
                              outs=[RSO.ap().opt()]).then_inc(sem("cc"), 1)
        gp.wait_ge(sem("cc"), 2)
    emit("gpsimd", f)
    bump("cc", 1)

    for hh in range(2):
        def f(sync, hh=hh, vo_need=(val("v_o") if hh == 1 else None)):
            if hh == 0:
                sync.wait_ge(sem("cc"), 2)
            if vo_need is not None:
                sync.wait_ge(sem("v_o"), vo_need)
            for k in range(NB):
                sync.dma_start(out=yc[k][:], in_=RSO.ap()[k * 128:(k + 1) * 128,
                                                          hh * TCC:(hh + 1) * TCC]
                               ).then_inc(sem("dr2"), 16)
        emit("sync", f)
        bump("dr2", 16 * NB)
        dr2_tok = val("dr2")
        for k in range(NB):
            def f(scalar, k=k, dr2_tok=dr2_tok, tln_need=(val("t_ln") if (k >= 1 or hh >= 0) else None)):
                if k == 0:
                    scalar.wait_ge(sem("dr2"), dr2_tok)
                if tln_need is not None:
                    scalar.wait_ge(sem("t_ln"), max(0, tln_need))
                scalar.activation(out=r1sq[0][:], in_=yc[k][:],
                                  func=AF.Square).then_inc(sem("s_sq"), 1)
            emit("scalar", f)
            bump("s_sq", 1)
            ssq_tok = val("s_sq")
            def f(tensor, k=k, dr2_tok=dr2_tok, ssq_tok=ssq_tok):
                if k == 0:
                    tensor.wait_ge(sem("dr2"), dr2_tok)
                tensor.matmul(ps[2][0:1, :], one16[:], yc[k][:],
                              start=(k == 0), stop=(k == NB - 1))
                tensor.wait_ge(sem("s_sq"), ssq_tok)
                mm = tensor.matmul(ps[3][0:1, :], one16[:], r1sq[0][:],
                                   start=(k == 0), stop=(k == NB - 1))
                mm.then_inc(sem("t_ln"), 1)
            emit("tensor", f)
            bump("t_ln", 1)
        tln_tok = val("t_ln")
        def f(scalar, tln_tok=tln_tok):
            scalar.wait_ge(sem("t_ln"), tln_tok)
            scalar.activation(out=mrow[:], in_=ps[2][0:1, :], func=AF.Copy, scale=1.0 / D)
            scalar.activation(out=s2row[:], in_=ps[3][0:1, :], func=AF.Copy, scale=1.0 / D)
            scalar.activation(out=msq[:], in_=mrow[:], func=AF.Square).then_inc(sem("s_row"), 1)
        emit("scalar", f)
        bump("s_row", 1)
        srow_tok = val("s_row")
        def f(vector, srow_tok=srow_tok, myv=val("v_var") + 1):
            vector.wait_ge(sem("s_row"), srow_tok)
            vector.tensor_tensor(out=varrow[:], in0=s2row[:], in1=msq[:],
                                 op=ALU.subtract).then_inc(sem("v_var"), 1)
            vector.wait_ge(sem("v_var"), myv)
            vector.tensor_scalar(out=msq[:], in0=varrow[:], scalar1=1e-5,
                                 scalar2=None, op0=ALU.add).then_inc(sem("v_var"), 1)
        emit("vector", f)
        bump("v_var", 2)
        vvar_tok = val("v_var")
        def f(scalar, vvar_tok=vvar_tok, mylog=val("s_row") + 1):
            scalar.wait_ge(sem("v_var"), vvar_tok)
            scalar.activation(out=varrow[:], in_=msq[:], func=AF.Ln,
                              scale=1.0).then_inc(sem("s_row"), 1)
            scalar.wait_ge(sem("s_row"), mylog)
            scalar.activation(out=rstdrow[:], in_=varrow[:], func=AF.Exp,
                              scale=-0.5).then_inc(sem("s_row"), 1)
        emit("scalar", f)
        bump("s_row", 2)
        vvar2_tok = val("s_row")
        def f(sync, srow_tok=vvar2_tok):
            sync.wait_ge(sem("s_row"), srow_tok)
            sync.dma_start(out=mrep[:], in_=mrow[0:1, :].unsqueeze(1).broadcast_to([1, 128, TCC])
                           ).then_inc(sem("dln"), 16)
            sync.dma_start(out=rstdrep[:], in_=rstdrow[0:1, :].unsqueeze(1).broadcast_to([1, 128, TCC])
                           ).then_inc(sem("dln"), 16)
        emit("sync", f)
        bump("dln", 32)
        dln_tok = val("dln")
        for k in range(NB):
            bump("v_o", 1)
            myo = val("v_o")
            def f(vector, k=k, dln_tok=dln_tok, myo=myo,
                  dout_need=(val("dout") if (k == 0 and hh == 1) else None)):
                if k == 0:
                    vector.wait_ge(sem("dln"), dln_tok)
                    if dout_need is not None:
                        vector.wait_ge(sem("dout"), dout_need)
                vector.tensor_tensor(out=tmp32[:], in0=yc[k][:], in1=mrep[:],
                                     op=ALU.subtract).then_inc(sem("v_o"), 1)
                vector.wait_ge(sem("v_o"), myo)
                vector.tensor_tensor(out=h1[k][:], in0=tmp32[:], in1=rstdrep[:],
                                     op=ALU.mult).then_inc(sem("v_o"), 1)
            emit("vector", f)
            bump("v_o", 1)
            vo_tok = val("v_o")
            def f(sync, k=k, hh=hh, vo_tok=vo_tok):
                sync.wait_ge(sem("v_o"), vo_tok)
                sync.dma_start(out=OUTQ[k * 128:(k + 1) * 128, hh * TCC:(hh + 1) * TCC],
                               in_=h1[k][:]).then_inc(sem("dout"), 16)
            emit("sync", f)
            bump("dout", 16)

    dout_all = val("dout")
    def f(sync, dout_all=dout_all):
        sync.wait_ge(sem("dout"), dout_all)
    emit("sync", f)
    def f(gp, dout_all=dout_all):
        gp.wait_ge(sem("dout"), dout_all)
    emit("gpsimd", f)

    stC.close()

    if phases < 4:
        ck = {1: ckpt1, 2: ckpt2, 3: ckpt3}[phases]
        for k in streams:
            streams[k] = streams[k][:ck[k]]

    with nc.Block() as block:
        @block.sync
        def _(sync):
            for fn in streams["sync"]:
                fn(sync)

        @block.scalar
        def _(scalar):
            for fn in streams["scalar"]:
                fn(scalar)

        @block.vector
        def _(vector):
            for fn in streams["vector"]:
                fn(vector)

        @block.tensor
        def _(tensor):
            for fn in streams["tensor"]:
                fn(tensor)

        @block.gpsimd
        def _(gpsimd):
            for fn in streams["gpsimd"]:
                fn(gpsimd)

    big.close()
    return nc


def kernel(L0_attn_x, L0_attn_norm_w, L0_attn_select_proj_w, L0_attn_select_proj_b,
           L0_attn_select_gate, L0_attn_x_proj_w, L0_attn_dt_proj_w, L0_attn_dt_proj_b,
           L0_attn_A_log, L0_attn_D_param, L0_attn_out_proj_w, L0_ffn_w_gate,
           L0_ffn_w_up, L0_ffn_w_down):
    f32 = np.float32
    x = np.asarray(L0_attn_x, f32)
    nw = np.asarray(L0_attn_norm_w, f32)
    Wsel = np.asarray(L0_attn_select_proj_w, f32)
    bsel = np.asarray(L0_attn_select_proj_b, f32)
    gate = np.asarray(L0_attn_select_gate, f32)
    Wxp = np.asarray(L0_attn_x_proj_w, f32)
    Wdt = np.asarray(L0_attn_dt_proj_w, f32)
    bdt = np.asarray(L0_attn_dt_proj_b, f32)
    A_log = np.asarray(L0_attn_A_log, f32)
    Dp = np.asarray(L0_attn_D_param, f32)
    Wout = np.asarray(L0_attn_out_proj_w, f32)
    Wg = np.asarray(L0_ffn_w_gate, f32)
    Wu = np.asarray(L0_ffn_w_up, f32)
    Wd = np.asarray(L0_ffn_w_down, f32)

    C = np.ascontiguousarray
    WoutH = C(Wout.T.reshape(NB, 128, NB, 128).transpose(1, 2, 0, 3).reshape(128, -1))
    xT = [C(x[b].T) for b in range(B)]
    XCb = [C(xT[b].reshape(NB, 128, NCC, TCC).transpose(2, 0, 1, 3).reshape(-1, TCC))
           for b in range(B)]
    nw16 = nw.reshape(NB, 128)
    gate16 = gate.reshape(NB, 128)
    bsel16 = bsel.reshape(NB, 128)
    DpC = C(Dp.reshape(NB, 128).T)
    Wsel4 = Wsel.T.reshape(NB, 128, NB, 128)
    Wxp4 = Wxp.T.reshape(NB, 128, 3, 128)
    WgT, WuT, WdT = Wg.T, Wu.T, Wd.T
    ones16 = np.ones((128, 1), BF)
    ones32 = np.ones((128, 1), f32)
    onesr16 = np.ones((1, 128), BF)
    onesr32 = np.ones((1, 128), f32)
    ident16 = np.eye(128, dtype=f32).astype(BF)

    in_maps = []
    for r in range(8):
        b, q = r // 4, r % 4
        perm = [(q * LOC + i) % NB for i in range(NB)]
        XBp = C(xT[b].reshape(NB, 128, NCA, TCA)[perm].transpose(2, 0, 1, 3).reshape(-1, TCA))
        WselH = C(Wsel4[perm][:, :, perm, :].transpose(1, 2, 0, 3).reshape(128, -1))
        WxpH = C(Wxp4[perm].transpose(1, 0, 2, 3).reshape(128, -1))
        WdtH = C(Wdt.T[:, q * 512:(q + 1) * 512])
        WgH = C(WgT[:, q * 2048:(q + 1) * 2048].reshape(NB, 128, HL, 128)
                .transpose(1, 2, 0, 3).reshape(128, -1))
        WuH = C(WuT[:, q * 2048:(q + 1) * 2048].reshape(NB, 128, HL, 128)
                .transpose(1, 2, 0, 3).reshape(128, -1))
        WdH = C(WdT[q * 2048:(q + 1) * 2048, :].reshape(HL, 128, NB, 128)
                .transpose(1, 2, 0, 3).reshape(128, -1))
        in_maps.append(dict(
            x_blk=XBp,
            xc_blk=XCb[b],
            wsel_h=WselH, wxp_h=WxpH, wdt_h=WdtH, wout_h=WoutH,
            wg_h=WgH, wu_h=WuH, wd_h=WdH,
            nw_c=C(nw16[perm].T), gate_c=C(gate16[perm].T), bsel_c=C(bsel16[perm].T),
            dp_c=DpC,
            bdt_c=C(bdt[q * 512:(q + 1) * 512].reshape(LOC, 128).T),
            alog_c=C(A_log[q * 512:(q + 1) * 512].reshape(LOC, 128, 128)
                     .transpose(1, 0, 2).reshape(128, LOC * 128)),
            ones16=ones16, ones32=ones32, onesr16=onesr16, onesr32=onesr32,
            ident16=ident16,
        ))

    nc = build_program()
    res = run_bass_kernel_spmd(nc, in_maps, core_ids=list(range(8)))
    globals()["_last_res"] = res

    out = np.empty((B, L, D), f32)
    for b in range(B):
        for q in range(4):
            oq = res.results[b * 4 + q]["out_q"]
            out[b, q * 1024:(q + 1) * 1024, :] = oq.T
    return out


if __name__ == "__main__":
    import time
    t0 = time.time()
    import sys
    nc = build_program(int(sys.argv[1]) if len(sys.argv) > 1 else 4)
    print(f"build ok: {time.time()-t0:.1f}s, n_instr={nc.n_instructions()}")
